# version 42
# baseline (speedup 1.0000x reference)
import numpy as np
import ml_dtypes
from contextlib import ExitStack

import concourse.bass as bass
import concourse.mybir as mybir
from concourse.bass_core_rust import IndirectOffsetOnAxis
from concourse.bass_utils import run_bass_kernel_spmd

F32 = mybir.dt.float32
BF16 = mybir.dt.bfloat16
I32 = mybir.dt.int32
AF = mybir.ActivationFunctionType
ALU = mybir.AluOpType
AX = mybir.AxisListType


class Cfg:
    def __init__(self, S=8192, D=1024, H=2816, E=8, CH=1024, QB=1024, KW=512,
                 CCAP=2432):
        self.S, self.D, self.H, self.E = S, D, H, E
        self.CH = CH
        self.QB = QB
        self.KW = KW
        self.CCAP = CCAP
        self.NJ = D // 128
        self.HC = H // 128
        self.NT = S // 128
        self.NG = S // 512
        self.NTB = (CH + 511) // 512
        self.TBW = [min(512, CH - 512 * i) for i in range(self.NTB)]
        self.T128 = CH // 128
        self.NDC = (D + 511) // 512
        self.DCW = [min(512, D - 512 * i) for i in range(self.NDC)]
        self.NQB = S // QB
        self.KSUB = KW // 128
        self.QR = QB // 128
        self.slabs = [(h, min(h + 2, self.HC)) for h in range(0, self.HC, 2)]
        self.NSLAB = len(self.slabs)
        self.SLOTS = E * CCAP
        self.SLOTS_PAD = self.SLOTS + 128
        widths = []
        r = CCAP
        while r > 0:
            w = min(CH, r)
            assert w % 128 == 0
            widths.append(w)
            r -= w
        self.NSUB = len(widths)
        self.NCHS = E * self.NSUB
        w0 = [widths[-1]] + widths[:-1]
        self.chunks = [(e, sub, w) for e in range(E)
                       for sub, w in enumerate(w0 if e == 0 else widths)]

    def KT(self, qb):
        return (qb + 1) * self.QB // self.KW


def build_program(cfg):
    S, D, H, E = cfg.S, cfg.D, cfg.H, cfg.E
    NJ, HC, NT, NG = cfg.NJ, cfg.HC, cfg.NT, cfg.NG
    CH, NTB, TBW, T128 = cfg.CH, cfg.NTB, cfg.TBW, cfg.T128
    NDC, DCW = cfg.NDC, cfg.DCW
    QB, KW, NQB, KSUB, QR = cfg.QB, cfg.KW, cfg.NQB, cfg.KSUB, cfg.QR
    NSLAB, CCAP, SLOTS, SLOTS_PAD = cfg.NSLAB, cfg.CCAP, cfg.SLOTS, cfg.SLOTS_PAD
    NCHS, NSUB = cfg.NCHS, cfg.NSUB

    nc = bass.Bass("TRN2", target_bir_lowering=False)

    x_d = nc.dram_tensor("x", [S, D], F32, kind="ExternalInput").ap()
    wq_d = nc.dram_tensor("wq", [D, D], F32, kind="ExternalInput").ap()
    wk_d = nc.dram_tensor("wk", [D, D], F32, kind="ExternalInput").ap()
    wv_d = nc.dram_tensor("wv", [D, D], F32, kind="ExternalInput").ap()
    wgt_d = nc.dram_tensor("wgt", [D, E], F32, kind="ExternalInput").ap()
    w1t_d = nc.dram_tensor("w1t", [E, D, H], F32, kind="ExternalInput").ap()
    w3t_d = nc.dram_tensor("w3t", [E, D, H], F32, kind="ExternalInput").ap()
    w2t_d = nc.dram_tensor("w2t", [E, H, D], F32, kind="ExternalInput").ap()
    w1b_d = nc.dram_tensor("w1b_s", [E, D, H], BF16, kind="Internal").ap()
    w3b_d = nc.dram_tensor("w3b_s", [E, D, H], BF16, kind="Internal").ap()
    w2b_d = nc.dram_tensor("w2b_s", [E, H, D], BF16, kind="Internal").ap()
    identb_d = nc.dram_tensor("identb", [128, 128], BF16, kind="ExternalInput").ap()
    uptri_d = nc.dram_tensor("uptri", [128, 128], BF16, kind="ExternalInput").ap()
    fconst_d = nc.dram_tensor("fconst", [128, 2], F32, kind="ExternalInput").ap()
    econst_d = nc.dram_tensor("econst", [128, 4, E], F32, kind="ExternalInput").ap()
    binm_d = nc.dram_tensor("binm", [KSUB, 128, KW], BF16, kind="ExternalInput").ap()
    binmT_d = nc.dram_tensor("binmT", [2, 128, 256], BF16, kind="ExternalInput").ap()
    ones_d = nc.dram_tensor("ones1", [128, 1], BF16, kind="ExternalInput").ap()
    out_d = nc.dram_tensor("out", [S, D], F32, kind="ExternalOutput").ap()

    xn_d = nc.dram_tensor("xn_s", [S, D], BF16, kind="Internal").ap()
    qT_d = nc.dram_tensor("qT_s", [D, S], BF16, kind="Internal").ap()
    kT_d = nc.dram_tensor("kT_s", [D, S], BF16, kind="Internal").ap()
    vtk_d = nc.dram_tensor("vtk_s", [S, D], BF16, kind="Internal").ap()
    xg_d = nc.dram_tensor("xg_s", [SLOTS_PAD, D], BF16, kind="Internal").ap()
    ys_d = nc.dram_tensor("ys_s", [SLOTS_PAD, D], F32, kind="Internal").ap()

    qT_r = qT_d.rearrange("(j p) t -> p j t", p=128)
    kT_r = kT_d.rearrange("(j p) t -> p j t", p=128)
    vtk_r = vtk_d.rearrange("(i p) d -> p i d", p=128)
    xg_r = xg_d.rearrange("(n p) d -> p n d", p=128)
    ys_r = ys_d.rearrange("(n p) d -> p n d", p=128)

    with ExitStack() as top:
        TE = top.enter_context
        ident_sb = TE(nc.sbuf_tensor("ident_sb", [128, 128], BF16))
        uptri_sb = TE(nc.sbuf_tensor("uptri_sb", [128, 128], BF16))
        fc_sb = TE(nc.sbuf_tensor("fc_sb", [128, 2], F32))
        econ_sb = TE(nc.sbuf_tensor("econ_sb", [128, 4, E], F32))
        binm_sb = TE(nc.sbuf_tensor("binm_sb", [128, KSUB, KW], BF16))
        cw_sb = TE(nc.sbuf_tensor("cw_sb", [128, NT, E], F32))
        s1f = TE(nc.sbuf_tensor("s1f", [128, NT], F32))
        s2f = TE(nc.sbuf_tensor("s2f", [128, NT], F32))
        s1i = TE(nc.sbuf_tensor("s1i", [128, NT], I32))
        s2i = TE(nc.sbuf_tensor("s2i", [128, NT], I32))
        cwA = TE(nc.sbuf_tensor("cwA", [128, NT], F32))
        cwB = TE(nc.sbuf_tensor("cwB", [128, NT], F32))
        rtb = TE(nc.sbuf_tensor("rtb", [128, 1], F32))
        pab = ExitStack()
        tstage = pab.enter_context(nc.sbuf_tensor("tstage", [128, 4, D], BF16))

        def sem(name):
            return TE(nc.semaphore(name))

        a_xl = [sem(f"a_xl{i}") for i in range(8)]
        a_vx = sem("a_vx")
        a_sq = sem("a_sq")
        a_vr = sem("a_vr")
        a_vc = sem("a_vc")
        a_sc = sem("a_sc")
        a_sr = sem("a_sr")
        a_tp = sem("a_tp")
        a_se = sem("a_se")
        a_mq = sem("a_mq")
        a_ve = sem("a_ve")
        a_mk = sem("a_mk")
        a_ke = sem("a_ke")
        a_mv = sem("a_mv")
        a_vse = sem("a_vse")
        a_mg = sem("a_mg")
        a_ge = sem("a_ge")
        a_vg = sem("a_vg")
        a_ld = sem("a_ld")
        awz = [sem(f"awz{i}") for i in range(4)]
        awm = sem("awm")
        a_xn = [sem(f"a_xn{i}") for i in range(4)]
        a_qs = [sem(f"a_qs{i}") for i in range(2)]
        a_ks = [sem(f"a_ks{i}") for i in range(2)]
        a_vs = [sem(f"a_vs{i}") for i in range(2)]
        b_rt = sem("b_rt")
        b_rv = sem("b_rv")
        b_rm = sem("b_rm")
        b_xns = [sem(f"b_xns{i}") for i in range(4)]
        b_sc = [sem(f"b_sc{i}") for i in range(4)]
        b_xgl = [sem(f"b_xgl{i}") for i in range(2)]
        b_mT2 = sem("b_mT2")
        b_se2 = sem("b_se2")
        b_w1f = [sem(f"b_w1f{i}") for i in range(3)]
        b_w3f = [sem(f"b_w3f{i}") for i in range(3)]
        b_w2f = sem("b_w2f")
        b_mG = sem("b_mG")
        b_sS = sem("b_sS")
        b_vH = sem("b_vH")
        b_mW = sem("b_mW")
        b_vY = sem("b_vY")
        b_yst = sem("b_yst")
        c_qtl = [sem(f"c_qtl{i}") for i in range(2)]
        c_vtf = [sem(f"c_vtf{i}") for i in range(4)]
        c_mS2 = sem("c_mS2")
        c_sE = sem("c_sE")
        c_vMm = sem("c_vMm")
        c_mO = sem("c_mO")
        c_vEr = sem("c_vEr")
        c_vEp = sem("c_vEp")
        c_xlf = [sem(f"c_xlf{i}") for i in range(2)]
        c_yga = [sem(f"c_yga{i}") for i in range(2)]
        c_ygb = [sem(f"c_ygb{i}") for i in range(2)]
        c_ost = [sem(f"c_ost{i}") for i in range(2)]
        c_kr = [sem(f"c_kr{i}") for i in range(4)]
        b_cast = [sem(f"b_cast{i}") for i in range(min(E, 4))]

        with ExitStack() as pa:
            PE = pa.enter_context
            xb = PE(nc.sbuf_tensor("xb", [128, 8, D], F32))
            sdump = PE(nc.sbuf_tensor("sdump", [128, 2, D], F32))
            ssum = PE(nc.sbuf_tensor("ssum", [128, 4], F32))
            ssq = PE(nc.sbuf_tensor("ssq", [128, 4], F32))
            mean4 = PE(nc.sbuf_tensor("mean4", [128, 4], F32))
            msq4 = PE(nc.sbuf_tensor("msq4", [128, 4], F32))
            m24 = PE(nc.sbuf_tensor("m24", [128, 4], F32))
            var4 = PE(nc.sbuf_tensor("var4", [128, 4], F32))
            ln4 = PE(nc.sbuf_tensor("ln4", [128, 4], F32))
            r4 = PE(nc.sbuf_tensor("r4", [128, 4], F32))
            negm4 = PE(nc.sbuf_tensor("negm4", [128, 4], F32))
            xnt = PE(nc.sbuf_tensor("xnt", [128, 8, D], BF16))
            xnTst = PE(nc.sbuf_tensor("xnTst", [128, 2, NJ, 512], BF16))
            QTst = PE(nc.sbuf_tensor("QTst", [128, 2, NJ, 512], BF16))
            KTst = PE(nc.sbuf_tensor("KTst", [128, 2, NJ, 512], BF16))
            Vst = PE(nc.sbuf_tensor("Vst", [128, 2, 4, D], BF16))
            WQsb = PE(nc.sbuf_tensor("WQsb", [128, NJ, D], BF16))
            WKsb = PE(nc.sbuf_tensor("WKsb", [128, NJ, D], BF16))
            WVsb = PE(nc.sbuf_tensor("WVsb", [128, NJ, D], BF16))
            WGsb = PE(nc.sbuf_tensor("WGsb", [128, NJ, E], BF16))
            zsb = PE(nc.sbuf_tensor("zsb", [128, 8192], BF16))
            selb4 = PE(nc.sbuf_tensor("selb4", [128, 4, E], F32))
            selbi4 = PE(nc.sbuf_tensor("selbi4", [128, 2, 4, E], I32))
            totb = PE(nc.sbuf_tensor("totb", [128, 4, E], BF16))
            selbE4 = PE(nc.sbuf_tensor("selbE4", [128, E, 4], F32))
            segm4 = PE(nc.sbuf_tensor("segm4", [128, E, 4], F32))
            rcs4 = PE(nc.sbuf_tensor("rcs4", [128, 2, E, 4], F32))
            offv4 = PE(nc.sbuf_tensor("offv4", [128, 4, E], F32))
            offc4 = PE(nc.sbuf_tensor("offc4", [128, 4, E], F32))
            selsl4 = PE(nc.sbuf_tensor("selsl4", [128, 4, E], F32))
            selsl24 = PE(nc.sbuf_tensor("selsl24", [128, 4, E], F32))
            eqm4 = PE(nc.sbuf_tensor("eqm4", [128, 4, E], F32))
            teq4 = PE(nc.sbuf_tensor("teq4", [128, 4, E], F32))
            ovmi4 = PE(nc.sbuf_tensor("ovmi4", [128, 4, E], I32))
            s1f4 = PE(nc.sbuf_tensor("s1f4", [128, 4, 1], F32))
            s2f4 = PE(nc.sbuf_tensor("s2f4", [128, 4, 1], F32))
            baseg = PE(nc.sbuf_tensor("baseg", [128, 1, E], F32))
            carry = PE(nc.sbuf_tensor("carry", [128, E], F32))
            etr4 = PE(nc.sbuf_tensor("etr4", [128, 4, E], F32))
            ge4 = PE(nc.sbuf_tensor("ge4", [128, 4, E], F32))
            sm4 = PE(nc.sbuf_tensor("sm4", [128, 4, E], F32))
            gge4 = PE(nc.sbuf_tensor("gge4", [128, 4, E], F32))
            gt4 = PE(nc.sbuf_tensor("gt4", [128, 4, E], F32))
            gmk4 = PE(nc.sbuf_tensor("gmk4", [128, 4, E], F32))
            gs4 = PE(nc.sbuf_tensor("gs4", [128, 4, 1], F32))
            grs4 = PE(nc.sbuf_tensor("grs4", [128, 4, 1], F32))
            gm14 = PE(nc.sbuf_tensor("gm14", [128, 4, 1], F32))
            gm24 = PE(nc.sbuf_tensor("gm24", [128, 4, 1], F32))
            gden4 = PE(nc.sbuf_tensor("gden4", [128, 4, 1], F32))
            grd4 = PE(nc.sbuf_tensor("grd4", [128, 4, 1], F32))
            psT = [PE(nc.psum_tensor(f"psT{i}", [128, 128], BF16)) for i in range(2)]
            psQK = [PE(nc.psum_tensor(f"psQK{i}", [128, 512], F32)) for i in range(2)]
            psV = [PE(nc.psum_tensor(f"psV{i}", [128, 512], F32)) for i in range(2)]
            psG = [PE(nc.psum_tensor(f"psG{i}", [128, 6, E], F32)) for i in range(2)]
            blkA = PE(nc.Block())

            NVC, VCW = NDC, DCW
            QSC = 1.0 / float(np.sqrt(D))

            def psqk_prior_wait(eng, g, phase, jq):
                u = 2 * NJ * g + (NJ if phase == "k" else 0) + jq
                if u < 2:
                    return
                up = u - 2
                gp, rem = divmod(up, 2 * NJ)
                if rem < NJ:
                    eng.wait_ge(a_ve, NJ * gp + rem + 1)
                else:
                    eng.wait_ge(a_ke, NJ * gp + (rem - NJ) + 1)

            KP = 4
            cast_pieces = {}
            for e_ in range(E):
                pl = []
                for i_ in range(KP):
                    h0, h1 = (H * i_) // KP, (H * (i_ + 1)) // KP
                    pl.append((w1b_d[e_][:, h0:h1], w1t_d[e_][:, h0:h1]))
                    pl.append((w3b_d[e_][:, h0:h1], w3t_d[e_][:, h0:h1]))
                    pl.append((w2b_d[e_][h0:h1, :], w2t_d[e_][h0:h1, :]))
                cast_pieces[e_] = pl

            @blkA.gpsimd
            def _(g):
                g.dma_start(out=ident_sb[:], in_=identb_d).then_inc(a_ld, 16)
                g.dma_start(out=uptri_sb[:], in_=uptri_d).then_inc(a_ld, 16)
                g.dma_start(out=fc_sb[:], in_=fconst_d).then_inc(a_ld, 16)
                g.dma_start(out=econ_sb[:], in_=econst_d).then_inc(a_ld, 16)
                g.dma_start(out=binm_sb[:], in_=binm_d.rearrange("m p k -> p m k")).then_inc(a_ld, 16)
                g.dma_start(out=WQsb[:], in_=wq_d.rearrange("(j p) q -> p j q", p=128)).then_inc(a_ld, 16)
                g.dma_start(out=WKsb[:], in_=wk_d.rearrange("(j p) q -> p j q", p=128)).then_inc(a_ld, 16)
                g.dma_start(out=WVsb[:], in_=wv_d.rearrange("(j p) q -> p j q", p=128)).then_inc(a_ld, 16)
                g.dma_start(out=WGsb[:], in_=wgt_d.rearrange("(j p) e -> p j e", p=128)).then_inc(a_ld, 16)
                g.memset(zsb[:], 0.0).then_inc(awm, 1)
                g.wait_ge(awm, 1)
                nrows = SLOTS_PAD // 128
                zw = 8192 // D
                nz = 0
                for n0 in range(0, nrows, zw):
                    w = min(zw, nrows - n0)
                    if nz >= 4:
                        g.wait_ge(awz[nz % 4], 16 * (nz // 4))
                    g.dma_start(out=xg_r[:, n0:n0 + w, :],
                                in_=zsb[:, :w * D]).then_inc(awz[nz % 4], 16)
                    nz += 1
                if nz >= 4:
                    g.wait_ge(awz[nz % 4], 16 * (nz // 4))
                g.dma_start(out=ys_r[:, SLOTS // 128:, :],
                            in_=zsb[:, :D]).then_inc(awz[nz % 4], 16)
                nz += 1
                for k4 in range(4):
                    cnt = (nz - k4 + 3) // 4
                    if cnt:
                        g.wait_ge(awz[k4], 16 * cnt)
                apieces = [(e_, oi) for e_ in range(min(2, E))
                           for oi in cast_pieces[e_]]
                nap = [0]
                for i in range(NT):
                    g.wait_ge(b_rt, i // 4 + 1)
                    g.wait_ge(b_xns[i % 4], 16 * (i // 4 + 1))
                    g.indirect_dma_start(out=xg_d, out_offset=IndirectOffsetOnAxis(s1i[:, i:i + 1], 0),
                                         in_=tstage[:, i % 4, :],
                                         in_offset=None).then_inc(b_sc[i % 4], 16)
                    g.indirect_dma_start(out=xg_d, out_offset=IndirectOffsetOnAxis(s2i[:, i:i + 1], 0),
                                         in_=tstage[:, i % 4, :],
                                         in_offset=None).then_inc(b_sc[i % 4], 16)
                    if i % 4 == 3 and nap[0] < len(apieces):
                        take = 2 if (i // 4) % 2 == 0 else 1
                        for _ in range(take):
                            if nap[0] < len(apieces):
                                e_, (o_, i_) = apieces[nap[0]]
                                g.dma_start(out=o_, in_=i_).then_inc(b_cast[e_ % 4], 16)
                                nap[0] += 1
                while nap[0] < len(apieces):
                    e_, (o_, i_) = apieces[nap[0]]
                    g.dma_start(out=o_, in_=i_).then_inc(b_cast[e_ % 4], 16)
                    nap[0] += 1

            @blkA.sync
            def _(sync):
                nst = [0]

                def stage_one(i):
                    sync.wait_ge(a_xn[i % 4], 16 * (i // 4 + 1))
                    if i >= 4:
                        sync.wait_ge(b_sc[i % 4], 32 * (i // 4))
                    sync.dma_start(out=tstage[:, i % 4, :],
                                   in_=xn_d[i * 128:(i + 1) * 128, :]).then_inc(b_xns[i % 4], 16)

                def stage_to(lim):
                    while nst[0] <= min(lim, NT - 1):
                        stage_one(nst[0])
                        nst[0] += 1

                def stores(g):
                    sl = g % 2
                    sync.wait_ge(a_ve, NJ * (g + 1))
                    sync.dma_start(out=qT_r[:, :, g * 512:(g + 1) * 512],
                                   in_=QTst[:, sl, :, :]).then_inc(a_qs[sl], 16)
                    sync.wait_ge(a_ke, NJ * (g + 1))
                    sync.dma_start(out=kT_r[:, :, g * 512:(g + 1) * 512],
                                   in_=KTst[:, sl, :, :]).then_inc(a_ks[sl], 16)
                    sync.wait_ge(a_vse, 4 * NVC * (g + 1))
                    sync.dma_start(out=vtk_r[:, g * 4:(g + 1) * 4, :],
                                   in_=Vst[:, sl, :, :]).then_inc(a_vs[sl], 16)

                for g in range(NG):
                    for i in range(4):
                        t = 4 * g + i
                        slot = t % 8
                        if t >= 8:
                            sync.wait_ge(a_vx, t - 7)
                            sync.wait_ge(a_sq, t - 7)
                        sync.dma_start(out=xb[:, slot, :],
                                       in_=x_d[t * 128:(t + 1) * 128, :]).then_inc(a_xl[slot], 16)
                    for u in range(max(0, 4 * g - 2), 4 * g + 2):
                        sync.wait_ge(a_vx, u + 1)
                        if u >= 4:
                            sync.wait_ge(a_xn[u % 4], 16 * (u // 4))
                        sync.dma_start(out=xn_d[u * 128:(u + 1) * 128, :],
                                       in_=xnt[:, u % 8, :]).then_inc(a_xn[u % 4], 16)
                        stage_to(u - 12)
                    if g >= 1:
                        stores(g - 1)
                for u in (NT - 2, NT - 1):
                    sync.wait_ge(a_vx, u + 1)
                    if u >= 4:
                        sync.wait_ge(a_xn[u % 4], 16 * (u // 4))
                    sync.dma_start(out=xn_d[u * 128:(u + 1) * 128, :],
                                   in_=xnt[:, u % 8, :]).then_inc(a_xn[u % 4], 16)
                    stage_to(u - 12)
                stores(NG - 1)
                stage_to(NT - 1)
                for s4 in range(4):
                    sync.wait_ge(a_xn[s4], 16 * (NT // 4))
                sync.wait_ge(a_qs[0], 16 * ((NG + 1) // 2))
                sync.wait_ge(a_qs[1], 16 * (NG // 2))
                sync.wait_ge(a_ks[0], 16 * ((NG + 1) // 2))
                sync.wait_ge(a_ks[1], 16 * (NG // 2))
                sync.wait_ge(a_vs[0], 16 * ((NG + 1) // 2))
                sync.wait_ge(a_vs[1], 16 * (NG // 2))

            r1done, r2done = {}, {}

            @blkA.vector
            def _(vector):
                def qevacs(g):
                    for jq in range(NJ):
                        if jq == 0 and g >= 2:
                            vector.wait_ge(a_qs[g % 2], 16 * (g // 2))
                        vector.wait_ge(a_mq, NJ * g + jq + 1)
                        vector.tensor_scalar_mul(out=QTst[:, g % 2, jq, :], in0=psQK[jq % 2][:],
                                                 scalar1=QSC).then_inc(a_ve, 1)

                def gating(g):
                    base = 12 * g

                    def gw(ins):
                        ins.then_inc(a_vg, 1)

                    def gwait(k):
                        vector.wait_ge(a_vg, base + k)

                    vector.wait_ge(a_ge, g + 1)
                    gw(vector.tensor_reduce(out=gs4[:], in_=ge4[:], axis=AX.X, op=ALU.add))
                    gwait(1)
                    gw(vector.reciprocal(out=grs4[:], in_=gs4[:]))
                    gwait(2)
                    gw(vector.tensor_tensor(out=sm4[:], in0=ge4[:],
                                            in1=grs4[:].broadcast_to([128, 4, E]), op=ALU.mult))
                    gwait(3)
                    gw(vector.tensor_reduce(out=gm14[:], in_=sm4[:], axis=AX.X, op=ALU.max))
                    gwait(4)
                    gw(vector.tensor_tensor(out=gge4[:], in0=sm4[:],
                                            in1=gm14[:].broadcast_to([128, 4, E]), op=ALU.is_ge))
                    gwait(5)
                    gw(vector.scalar_tensor_tensor(out=gt4[:], in0=gge4[:], scalar=-1e30,
                                                   in1=sm4[:], op0=ALU.mult, op1=ALU.add))
                    gwait(6)
                    gw(vector.tensor_reduce(out=gm24[:], in_=gt4[:], axis=AX.X, op=ALU.max))
                    gwait(7)
                    gw(vector.scalar_tensor_tensor(out=gden4[:], in0=gm14[:], scalar=1e-20,
                                                   in1=gm24[:], op0=ALU.add, op1=ALU.add))
                    gwait(8)
                    gw(vector.reciprocal(out=grd4[:], in_=gden4[:]))
                    gwait(7)
                    gw(vector.tensor_tensor(out=gmk4[:], in0=sm4[:],
                                            in1=gm24[:].broadcast_to([128, 4, E]), op=ALU.is_ge))
                    gwait(10)
                    gw(vector.tensor_mul(out=gt4[:], in0=sm4[:], in1=gmk4[:]))
                    gwait(11)
                    gw(vector.tensor_tensor(out=cw_sb[:, 4 * g:4 * g + 4, :], in0=gt4[:],
                                            in1=grd4[:].broadcast_to([128, 4, E]), op=ALU.mult))

                rv = [0]

                def rop(ins):
                    rv[0] += 1
                    ins.then_inc(b_rv, 1)

                def rwait():
                    vector.wait_ge(b_rv, rv[0])

                def routing1(g):
                    sl = g % 2
                    vector.wait_ge(a_vg, 12 * (g + 1))
                    rop(vector.tensor_scalar(out=selb4[:], in0=cw_sb[:, 4 * g:4 * g + 4, :],
                                             scalar1=0.0, scalar2=None, op0=ALU.is_gt))
                    rwait()
                    rop(vector.tensor_copy(out=selbi4[:, sl], in_=selb4[:]))
                    rop(vector.tensor_copy(out=selbE4[:].rearrange("p e t -> p t e"),
                                           in_=selb4[:]))
                    rwait()
                    rop(vector.tensor_tensor_scan(out=rcs4[:, sl].rearrange("p e t -> p (e t)"),
                                                  data0=segm4[:].rearrange("p e t -> p (e t)"),
                                                  data1=selbE4[:].rearrange("p e t -> p (e t)"),
                                                  initial=0.0, op0=ALU.mult, op1=ALU.add))
                    rwait()
                    if g >= 4:
                        vector.wait_ge(b_rm, g - 3)
                    rop(vector.tensor_copy(out=totb[:, g % 4], in_=rcs4[:, sl, :, 3]))
                    r1done[g] = rv[0]

                def routing2(gg):
                    sl = gg % 2
                    vector.wait_ge(b_rm, gg + 1)
                    rop(vector.tensor_tensor(out=baseg[:, 0, :], in0=psG[gg % 2][:, 4, :],
                                             in1=carry[:], op=ALU.add))
                    rwait()
                    rop(vector.scalar_tensor_tensor(out=offv4[:],
                                                    in0=rcs4[:, sl].rearrange("p e t -> p t e"),
                                                    scalar=-1.0,
                                                    in1=baseg[:].broadcast_to([128, 4, E]),
                                                    op0=ALU.add, op1=ALU.add))
                    rop(vector.tensor_tensor(out=carry[:], in0=carry[:], in1=psG[gg % 2][:, 5, :],
                                             op=ALU.add))
                    rwait()
                    rop(vector.tensor_tensor(out=ovmi4[:], in0=offv4[:],
                                             in1=econ_sb[:, 1:2, :].broadcast_to([128, 4, E]),
                                             op=ALU.is_ge))
                    rop(vector.tensor_copy(out=offc4[:], in_=offv4[:]))
                    rwait()
                    rop(vector.copy_predicated(out=offc4[:], mask=ovmi4[:], data=etr4[:]))
                    rop(vector.memset(selsl4[:], 1e9))
                    rwait()
                    rop(vector.copy_predicated(out=selsl4[:], mask=selbi4[:, sl], data=offc4[:]))
                    rwait()
                    rop(vector.tensor_reduce(out=s1f4[:], in_=selsl4[:], axis=AX.X, op=ALU.min))
                    rwait()
                    rop(vector.tensor_tensor(out=eqm4[:], in0=selsl4[:],
                                             in1=s1f4[:].broadcast_to([128, 4, E]), op=ALU.is_le))
                    rwait()
                    rop(vector.tensor_mul(out=teq4[:], in0=cw_sb[:, 4 * gg:4 * gg + 4, :],
                                          in1=eqm4[:]))
                    rop(vector.scalar_tensor_tensor(out=selsl24[:], in0=eqm4[:], scalar=1e9,
                                                    in1=selsl4[:], op0=ALU.mult, op1=ALU.add))
                    rwait()
                    rop(vector.tensor_reduce(out=cwA[:, 4 * gg:4 * gg + 4], in_=teq4[:],
                                             axis=AX.X, op=ALU.add))
                    rop(vector.tensor_reduce(out=s2f4[:], in_=selsl24[:], axis=AX.X, op=ALU.min))
                    rwait()
                    rop(vector.tensor_tensor(out=eqm4[:], in0=selsl24[:],
                                             in1=s2f4[:].broadcast_to([128, 4, E]), op=ALU.is_le))
                    rwait()
                    rop(vector.tensor_mul(out=teq4[:], in0=cw_sb[:, 4 * gg:4 * gg + 4, :],
                                          in1=eqm4[:]))
                    rwait()
                    rop(vector.tensor_reduce(out=cwB[:, 4 * gg:4 * gg + 4], in_=teq4[:],
                                             axis=AX.X, op=ALU.add))
                    rop(vector.tensor_copy(out=s1i[:, 4 * gg:4 * gg + 4], in_=s1f4[:, :, 0]))
                    rop(vector.tensor_copy(out=s2i[:, 4 * gg:4 * gg + 4], in_=s2f4[:, :, 0]))
                    rwait()
                    vector.memset(rtb[:], 0.0).then_inc(b_rt, 1)
                    r2done[gg] = rv[0]

                vector.wait_ge(a_ld, 16 * 9)
                rop(vector.tensor_copy(out=carry[:], in_=econ_sb[:, 0, :]))
                rop(vector.memset(segm4[:], 1.0))
                rop(vector.memset(etr4[:], float(SLOTS)))
                rwait()
                rop(vector.memset(segm4[:, :, 0:1], 0.0))
                rwait()

                def lnchain(tr):
                    for i in range(4):
                        t = 4 * tr + i
                        slot = t % 8
                        vector.wait_ge(a_xl[slot], 16 * (t // 8 + 1))
                        if i == 0 and tr >= 1:
                            vector.wait_ge(a_vx, 4 * tr)
                            vector.wait_ge(a_sc, tr)
                        ins = vector.reduce_sum(out=ssum[:, i:i + 1], in_=xb[:, slot, :], axis=AX.X)
                        if i == 3:
                            ins.then_inc(a_vr, 1)
                    vector.wait_ge(a_vr, tr + 1)
                    vector.wait_ge(a_sq, 4 * (tr + 1))
                    vector.tensor_scalar_mul(out=mean4[:], in0=ssum[:], scalar1=1.0 / D).then_inc(a_vc, 1)
                    vector.tensor_scalar_mul(out=msq4[:], in0=ssq[:], scalar1=1.0 / D)
                    vector.wait_ge(a_vc, 4 * tr + 1)
                    vector.tensor_mul(out=m24[:], in0=mean4[:], in1=mean4[:]).then_inc(a_vc, 1)
                    vector.wait_ge(a_vc, 4 * tr + 2)
                    vector.scalar_tensor_tensor(out=var4[:], in0=msq4[:], scalar=1e-5, in1=m24[:],
                                                op0=ALU.add, op1=ALU.subtract).then_inc(a_vc, 1)
                    vector.wait_ge(a_sr, tr + 1)
                    vector.scalar_tensor_tensor(out=negm4[:], in0=mean4[:], scalar=-1.0, in1=r4[:],
                                                op0=ALU.mult, op1=ALU.mult).then_inc(a_vc, 1)
                    vector.wait_ge(a_vc, 4 * tr + 4)
                    for i in range(4):
                        t = 4 * tr + i
                        slot = t % 8
                        if t >= 8:
                            vector.wait_ge(a_tp, NJ * (t - 7))
                            vector.wait_ge(a_xn[t % 4], 16 * ((t - 4) // 4))
                        vector.tensor_scalar(out=xnt[:, slot, :], in0=xb[:, slot, :],
                                             scalar1=r4[:, i:i + 1], scalar2=negm4[:, i:i + 1],
                                             op0=ALU.mult, op1=ALU.add).then_inc(a_vx, 1)

                lnchain(0)
                for g in range(NG):
                    if g >= 1:
                        gating(g - 1)
                    qevacs(g)
                    if g + 1 < NG:
                        lnchain(g + 1)
                    if g >= 3:
                        routing2(g - 3)
                    if g >= 1:
                        routing1(g - 1)
                gating(NG - 1)
                routing2(NG - 3)
                routing1(NG - 1)
                routing2(NG - 2)
                routing2(NG - 1)
                pass

            @blkA.scalar
            def _(scalar):
                def gexps(g):
                    scalar.wait_ge(a_mg, g + 1)
                    if g >= 1:
                        scalar.wait_ge(a_vg, 12 * (g - 1) + 3)
                    scalar.activation(out=ge4[:], in_=psG[g % 2][:, 0:4, :],
                                      func=AF.Exp).then_inc(a_ge, 1)

                def sqln(tr):
                    for i in range(4):
                        t = 4 * tr + i
                        slot = t % 8
                        scalar.wait_ge(a_xl[slot], 16 * (t // 8 + 1))
                        if t >= 2:
                            scalar.wait_ge(a_sq, t - 1)
                        if i == 0 and tr >= 1:
                            scalar.wait_ge(a_vx, 4 * tr)
                        scalar.activation(out=sdump[:, t % 2, :], in_=xb[:, slot, :], func=AF.Square,
                                          accum_out=ssq[:, i:i + 1]).then_inc(a_sq, 1)
                    scalar.wait_ge(a_vc, 4 * tr + 3)
                    scalar.activation(out=ln4[:], in_=var4[:], func=AF.Ln).then_inc(a_sc, 1)
                    scalar.wait_ge(a_sc, tr + 1)
                    if tr >= 1:
                        scalar.wait_ge(a_vx, 4 * tr)
                    scalar.activation(out=r4[:], in_=ln4[:], func=AF.Exp, scale=fc_sb[:, 0:1]).then_inc(a_sr, 1)

                for g in range(NG):
                    sl = g % 2
                    if g == 0:
                        scalar.wait_ge(a_ld, 16 * 9)
                        sqln(0)
                    if g >= 1:
                        gexps(g - 1)
                    for i in range(4):
                        t = 4 * g + i
                        for j in range(NJ):
                            idx = t * NJ + j
                            if i == 0 and j == 0 and g >= 2:
                                scalar.wait_ge(a_mg, g - 1)
                            scalar.wait_ge(a_tp, idx + 1)
                            scalar.activation(out=xnTst[:, sl, j, i * 128:(i + 1) * 128],
                                              in_=psT[idx % 2][:], func=AF.Identity).then_inc(a_se, 1)
                    for jq in range(NJ):
                        if jq == 0 and g >= 2:
                            scalar.wait_ge(a_ks[sl], 16 * (g // 2))
                        scalar.wait_ge(a_mk, NJ * g + jq + 1)
                        scalar.activation(out=KTst[:, sl, jq, :], in_=psQK[jq % 2][:],
                                          func=AF.Identity).then_inc(a_ke, 1)
                    for i2 in range(4):
                        for dv in range(NVC):
                            k = i2 * NVC + dv
                            if k == 0 and g >= 2:
                                scalar.wait_ge(a_vs[sl], 16 * (g // 2))
                            scalar.wait_ge(a_mv, 4 * NVC * g + k + 1)
                            scalar.activation(out=Vst[:, sl, i2, dv * 512:dv * 512 + VCW[dv]],
                                              in_=psV[k % 2][:, :VCW[dv]], func=AF.Identity).then_inc(a_vse, 1)
                    if g + 1 < NG:
                        sqln(g + 1)
                gexps(NG - 1)

            @blkA.tensor
            def _(tensor):
                def routing_mms(gg):
                    tensor.wait_ge(b_rv, r1done[gg])
                    if gg >= 2:
                        tensor.wait_ge(b_rv, r2done[gg - 2])
                    tensor.wait_ge(a_ge, gg + 1)
                    tensor.matmul(out=psG[gg % 2][:, 4, :], lhsT=uptri_sb[:], rhs=totb[:, gg % 4],
                                  start=True, stop=True)
                    tensor.matmul(out=psG[gg % 2][:, 5, :], lhsT=binm_sb[:, KSUB - 1, 0:128],
                                  rhs=totb[:, gg % 4],
                                  start=True, stop=True).then_inc(b_rm, 1)

                tensor.wait_ge(a_ld, 16 * 9)
                for g in range(NG):
                    sl = g % 2
                    if g >= 3:
                        routing_mms(g - 3)
                    for i in range(4):
                        t = 4 * g + i
                        tensor.wait_ge(a_vx, t + 1)
                        for j in range(NJ):
                            idx = t * NJ + j
                            if idx >= 2:
                                tensor.wait_ge(a_se, idx - 1)
                            tensor.transpose(out=psT[idx % 2][:], in_=xnt[:, t % 8, j * 128:(j + 1) * 128],
                                             identity=ident_sb[:]).then_inc(a_tp, 1)
                    tensor.wait_ge(a_se, 4 * NJ * (g + 1))
                    if g >= 2:
                        tensor.wait_ge(a_ge, g - 1)
                    for i2 in range(4):
                        for j in range(NJ):
                            mi = tensor.matmul(out=psG[g % 2][:, i2, :],
                                               lhsT=xnTst[:, sl, j, i2 * 128:(i2 + 1) * 128],
                                               rhs=WGsb[:, j, :], start=(j == 0), stop=(j == NJ - 1))
                    mi.then_inc(a_mg, 1)
                    for jq in range(NJ):
                        psqk_prior_wait(tensor, g, "q", jq)
                        for j in range(NJ):
                            mi = tensor.matmul(out=psQK[jq % 2][:], lhsT=WQsb[:, j, jq * 128:(jq + 1) * 128],
                                               rhs=xnTst[:, sl, j, :], start=(j == 0), stop=(j == NJ - 1))
                        mi.then_inc(a_mq, 1)
                    for jq in range(NJ):
                        psqk_prior_wait(tensor, g, "k", jq)
                        for j in range(NJ):
                            mi = tensor.matmul(out=psQK[jq % 2][:], lhsT=WKsb[:, j, jq * 128:(jq + 1) * 128],
                                               rhs=xnTst[:, sl, j, :], start=(j == 0), stop=(j == NJ - 1))
                        mi.then_inc(a_mk, 1)
                    for i2 in range(4):
                        for dv in range(NVC):
                            vix = 4 * NVC * g + i2 * NVC + dv
                            if vix >= 2:
                                tensor.wait_ge(a_vse, vix - 1)
                            for j in range(NJ):
                                mi = tensor.matmul(out=psV[(i2 * NVC + dv) % 2][:, :VCW[dv]],
                                                   lhsT=xnTst[:, sl, j, i2 * 128:(i2 + 1) * 128],
                                                   rhs=WVsb[:, j, dv * 512:dv * 512 + VCW[dv]],
                                                   start=(j == 0), stop=(j == NJ - 1))
                            mi.then_inc(a_mv, 1)
                routing_mms(NG - 3)
                routing_mms(NG - 2)
                routing_mms(NG - 1)

        with ExitStack() as pb:
            PB = pb.enter_context
            NSL = 3
            xgT = PB(nc.sbuf_tensor("xgT", [128, 2, NJ, CH], BF16))
            w1r = PB(nc.sbuf_tensor("w1r", [128, NSL, NJ, 256], BF16))
            w3r = PB(nc.sbuf_tensor("w3r", [128, NSL, NJ, 256], BF16))
            w2sb = PB(nc.sbuf_tensor("w2sb", [128, HC, D], BF16))
            hT = PB(nc.sbuf_tensor("hT", [128, HC, CH], BF16))
            s1b = PB(nc.sbuf_tensor("s1b", [128, 2, 512], BF16))
            ysb = PB(nc.sbuf_tensor("ysb", [128, T128, D], F32))
            pg1 = [PB(nc.psum_tensor(f"pg1_{i}", [128, 512], F32)) for i in range(2)]
            pg3 = [PB(nc.psum_tensor(f"pg3_{i}", [128, 512], F32)) for i in range(2)]
            pw = [PB(nc.psum_tensor(f"pw_{i}", [128, 512], F32)) for i in range(2)]
            psT2 = [PB(nc.psum_tensor(f"psT2_{i}", [128, 128], BF16)) for i in range(2)]
            blkB = PB(nc.Block())

            NBW = min(NDC, 2)
            CHWs = [w for (_, _, w) in cfg.chunks]
            T128c = [w // 128 for w in CHWs]
            NTBc = [(w + 511) // 512 for w in CHWs]
            TBWc = [[min(512, w - 512 * i) for i in range(NTBc[c])]
                    for c, w in enumerate(CHWs)]
            tilebase, gbase, wbase = [0], [0], [0]
            for c in range(NCHS):
                tilebase.append(tilebase[-1] + T128c[c])
                gbase.append(gbase[-1] + HC * NTBc[c])
                wbase.append(wbase[-1] + T128c[c] * NDC)
            w1b_r = [w1b_d[e].rearrange("(j p) h -> p j h", p=128) for e in range(E)]
            w3b_r = [w3b_d[e].rearrange("(j p) h -> p j h", p=128) for e in range(E)]
            w2b_r = [w2b_d[e].rearrange("(hc p) d -> p hc d", p=128) for e in range(E)]

            def gidx(c, hc, tb):
                return gbase[c] + hc * NTBc[c] + tb

            def widx(c, t, dc):
                return wbase[c] + t * NDC + dc

            slab_events = []
            for c in range(NCHS):
                e = cfg.chunks[c][0]
                for s, (lo, hi) in enumerate(cfg.slabs):
                    slab_events.append((c, e, s, lo, hi))

            def slab_done_count(si):
                (c, e, s, lo, hi) = slab_events[si]
                return 2 * (gidx(c, hi - 1, NTBc[c] - 1) + 1)

            tr_items, KPT = {}, {}
            for c in range(NCHS):
                if c + 1 < NCHS:
                    items = [(g8, j) for g8 in range(tilebase[c + 1], tilebase[c + 2])
                             for j in range(NJ)]
                else:
                    items = []
                tr_items[c] = items
                npts = max(1, (NSLAB * 2 * NTBc[c]) * 3 // 4)
                KPT[c] = (len(items) + npts - 1) // npts if items else 0


            @blkB.vector
            def _(vector):
                for c in range(NCHS):
                    if c >= 1:
                        vector.wait_ge(b_mW, wbase[c])
                    for hc in range(HC):
                        for tb in range(NTBc[c]):
                            gx = gidx(c, hc, tb)
                            vector.wait_ge(b_mG, 2 * gx + 2)
                            vector.wait_ge(b_sS, gx + 1)
                            vector.tensor_tensor(out=hT[:, hc, tb * 512:tb * 512 + TBWc[c][tb]],
                                                 in0=s1b[:, gx % 2, :TBWc[c][tb]],
                                                 in1=pg3[gx % 2][:, :TBWc[c][tb]],
                                                 op=ALU.mult).then_inc(b_vH, 1)
                    if c >= 1:
                        vector.wait_ge(b_yst, 16 * c)
                        vector.wait_ge(b_vY, wbase[c])
                    for t in range(T128c[c]):
                        for dc in range(NDC):
                            wx = widx(c, t, dc)
                            vector.wait_ge(b_mW, wx + 1)
                            vector.tensor_copy(out=ysb[:, t, dc * 512:dc * 512 + DCW[dc]],
                                               in_=pw[dc % NBW][:, :DCW[dc]]).then_inc(b_vY, 1)

            @blkB.tensor
            def _(tensor):
                def one_tr(g8, j):
                    tix = g8 * NJ + j
                    if j == 0:
                        tensor.wait_ge(b_xgl[g8 % 2], 16 * (g8 // 2 + 1))
                    if tix >= 2:
                        tensor.wait_ge(b_se2, tix - 1)
                    tensor.transpose(out=psT2[tix % 2][:],
                                     in_=tstage[:, g8 % 4, j * 128:(j + 1) * 128],
                                     identity=ident_sb[:]).then_inc(b_mT2, 1)

                for c in range(NCHS):
                    e = cfg.chunks[c][0]
                    if c == 0:
                        for g8 in range(tilebase[0], tilebase[1]):
                            for j in range(NJ):
                                one_tr(g8, j)
                    tensor.wait_ge(b_se2, NJ * tilebase[c + 1])
                    ti = [0]

                    def drip():
                        items = tr_items[c]
                        for _ in range(KPT[c]):
                            if ti[0] < len(items):
                                g8, j = items[ti[0]]
                                ti[0] += 1
                                one_tr(g8, j)

                    for s, (lo, hi) in enumerate(cfg.slabs):
                        si = c * NSLAB + s
                        slot = si % NSL
                        tensor.wait_ge(b_w1f[slot], 16 * (si // NSL + 1))
                        tensor.wait_ge(b_w3f[slot], 16 * (si // NSL + 1))
                        for hc in range(lo, hi):
                            off = (hc - lo) * 128
                            for tb in range(NTBc[c]):
                                gx = gidx(c, hc, tb)
                                if gx >= 2:
                                    tensor.wait_ge(b_sS, gx - 1)
                                for j in range(NJ):
                                    mi = tensor.matmul(out=pg1[gx % 2][:, :TBWc[c][tb]],
                                                       lhsT=w1r[:, slot, j, off:off + 128],
                                                       rhs=xgT[:, c % 2, j, tb * 512:tb * 512 + TBWc[c][tb]],
                                                       start=(j == 0), stop=(j == NJ - 1))
                                mi.then_inc(b_mG, 1)
                                if gx >= 2:
                                    tensor.wait_ge(b_vH, gx - 1)
                                for j in range(NJ):
                                    mi = tensor.matmul(out=pg3[gx % 2][:, :TBWc[c][tb]],
                                                       lhsT=w3r[:, slot, j, off:off + 128],
                                                       rhs=xgT[:, c % 2, j, tb * 512:tb * 512 + TBWc[c][tb]],
                                                       start=(j == 0), stop=(j == NJ - 1))
                                mi.then_inc(b_mG, 1)
                                drip()
                    while ti[0] < len(tr_items[c]):
                        g8, j = tr_items[c][ti[0]]
                        ti[0] += 1
                        one_tr(g8, j)
                    tensor.wait_ge(b_vH, gbase[c + 1])
                    tensor.wait_ge(b_w2f, 32 * (e + 1))
                    for t in range(T128c[c]):
                        for dc in range(NDC):
                            wx = widx(c, t, dc)
                            if wx >= NBW:
                                tensor.wait_ge(b_vY, wx - NBW + 1)
                            for hc in range(HC):
                                mi = tensor.matmul(out=pw[dc % NBW][:, :DCW[dc]],
                                                   lhsT=hT[:, hc, t * 128:(t + 1) * 128],
                                                   rhs=w2sb[:, hc, dc * 512:dc * 512 + DCW[dc]],
                                                   start=(hc == 0), stop=(hc == HC - 1))
                            mi.then_inc(b_mW, 1)

            @blkB.scalar
            def _(scalar):
                def one_ev(cc, g8, j):
                    i = g8 - tilebase[cc]
                    tix = g8 * NJ + j
                    if i == 0 and j == 0 and cc >= 2:
                        scalar.wait_ge(b_mG, 2 * gbase[cc - 1])
                    scalar.wait_ge(b_mT2, tix + 1)
                    scalar.activation(out=xgT[:, cc % 2, j, i * 128:(i + 1) * 128],
                                      in_=psT2[tix % 2][:], func=AF.Identity).then_inc(b_se2, 1)

                for c in range(NCHS):
                    if c == 0:
                        for g8 in range(tilebase[0], tilebase[1]):
                            for j in range(NJ):
                                one_ev(0, g8, j)
                    si_ = [0]

                    def sdrip():
                        items = tr_items[c]
                        for _ in range(KPT[c]):
                            if si_[0] < len(items):
                                g8, j = items[si_[0]]
                                si_[0] += 1
                                one_ev(c + 1, g8, j)

                    for hc in range(HC):
                        for tb in range(NTBc[c]):
                            gx = gidx(c, hc, tb)
                            scalar.wait_ge(b_mG, 2 * gx + 1)
                            if gx >= 2:
                                scalar.wait_ge(b_vH, gx - 1)
                            scalar.activation(out=s1b[:, gx % 2, :TBWc[c][tb]],
                                              in_=pg1[gx % 2][:, :TBWc[c][tb]],
                                              func=AF.Silu).then_inc(b_sS, 1)
                            sdrip()
                            if (cfg.chunks[c][1] == 0 and tb == 0
                                    and hc == max(0, HC - 8)):
                                e_ = cfg.chunks[c][0]
                                if c >= 1:
                                    scalar.wait_ge(b_mW, wbase[c])
                                half = (HC + 1) // 2
                                scalar.dma_start(out=w2sb[:, :half, :],
                                                 in_=w2b_r[e_][:, :half, :]).then_inc(b_w2f, 16)
                                scalar.dma_start(out=w2sb[:, half:, :],
                                                 in_=w2b_r[e_][:, half:, :]).then_inc(b_w2f, 16)
                    while si_[0] < len(tr_items[c]):
                        g8, j = tr_items[c][si_[0]]
                        si_[0] += 1
                        one_ev(c + 1, g8, j)

            @blkB.gpsimd
            def _(g):
                half = (HC + 1) // 2
                npc = {e_: (len(cast_pieces[e_]) if e_ < 2 else 0) for e_ in range(E)}
                seen = set()

                def drip(e_):
                    if e_ < E and npc[e_] < len(cast_pieces[e_]):
                        o_, i_ = cast_pieces[e_][npc[e_]]
                        g.dma_start(out=o_, in_=i_).then_inc(b_cast[e_ % 4], 16)
                        npc[e_] += 1
                        return True
                    return False

                def tile_load(g8):
                    if g8 >= 4:
                        g.wait_ge(b_mT2, NJ * (g8 - 3))
                    g.dma_start(out=tstage[:, g8 % 4, :],
                                in_=xg_d[g8 * 128:(g8 + 1) * 128, :]
                                ).then_inc(b_xgl[g8 % 2], 16)

                for si, (c, e, s, lo, hi) in enumerate(slab_events):
                    slot = si % NSL
                    wdt = (hi - lo) * 128
                    if e not in seen:
                        seen.add(e)
                        while drip(e):
                            pass
                        nslot = sum(len(cast_pieces[e2]) for e2 in range(e + 1)
                                    if e2 % 4 == e % 4)
                        g.wait_ge(b_cast[e % 4], 16 * nslot)
                    if si >= NSL:
                        g.wait_ge(b_mG, slab_done_count(si - NSL))
                    g.dma_start(out=w1r[:, slot, :, :wdt],
                                in_=w1b_r[e][:, :, lo * 128:hi * 128]).then_inc(b_w1f[slot], 16)
                    g.dma_start(out=w3r[:, slot, :, :wdt],
                                in_=w3b_r[e][:, :, lo * 128:hi * 128]).then_inc(b_w3f[slot], 16)
                    if s % 2 == 0:
                        drip(e + 2)
                    if si == 0:
                        for s4 in range(4):
                            g.wait_ge(b_sc[s4], 32 * ((NT - s4 + 3) // 4))
                        for g8 in range(tilebase[0], tilebase[1]):
                            tile_load(g8)
                    if c + 1 < NCHS and s < T128c[c + 1]:
                        tile_load(tilebase[c + 1] + s)

            @blkB.sync
            def _(sync):
                for c in range(NCHS):
                    sync.wait_ge(b_vY, wbase[c + 1])
                    sync.dma_start(out=ys_r[:, tilebase[c]:tilebase[c + 1], :],
                                   in_=ysb[:, :T128c[c], :]).then_inc(b_yst, 16)
                sync.wait_ge(b_yst, 16 * NCHS)

        pab.close()
        with ExitStack() as pc:
            PC = pc.enter_context
            QB2 = 256
            QI = QB2 // 128
            NQB2 = S // QB2
            VD = 6
            QTsb = PC(nc.sbuf_tensor("QTsb", [128, 2, NJ, QB2], BF16))
            kres = PC(nc.sbuf_tensor("kres", [128, NJ, S], BF16))
            vtr = PC(nc.sbuf_tensor("vtr", [128, VD, D], BF16))
            PTs = PC(nc.sbuf_tensor("PTs", [128, 4, QB2], BF16))
            binmT_sb = PC(nc.sbuf_tensor("binmT_sb", [128, 2, QB2], BF16))
            ones_sb = PC(nc.sbuf_tensor("ones_sb", [128, 1], BF16))
            recL = PC(nc.sbuf_tensor("recL", [128, 2], F32))
            xtl = PC(nc.sbuf_tensor("xtl", [128, 2, D], F32))
            ygA = PC(nc.sbuf_tensor("ygA", [128, 2, D], F32))
            ygB = PC(nc.sbuf_tensor("ygB", [128, 2, D], F32))
            ot1 = PC(nc.sbuf_tensor("ot1", [128, 2, D], F32))
            ot2 = PC(nc.sbuf_tensor("ot2", [128, D], F32))
            otl = PC(nc.sbuf_tensor("otl", [128, 2, D], F32))
            psS = [PC(nc.psum_tensor(f"psS_{i}", [128, QB2], F32)) for i in range(2)]
            psO = [PC(nc.psum_tensor(f"psO_{i}", [128, 512], F32)) for i in range(4)]
            psL = [PC(nc.psum_tensor(f"psL_{i}", [128, 8], F32)) for i in range(2)]
            blkC = PC(nc.Block())

            pairs = []
            for qb in range(NQB2):
                for kt in range(2 * (qb + 1)):
                    pairs.append((qb, kt))
            p_ord = {ev: i for i, ev in enumerate(pairs)}
            pairs_end = {}
            tot = 0
            for qb in range(NQB2):
                tot += 2 * (qb + 1)
                pairs_end[qb] = tot
            kt_slot = {ev: i % VD for i, ev in enumerate(pairs)}
            kt_sem = {ev: i % 4 for i, ev in enumerate(pairs)}
            kt_fill = {ev: 16 * (i // 4 + 1) for i, ev in enumerate(pairs)}
            mask_cum = []
            cnt = 0
            for (qb, kt) in pairs:
                if kt >= 2 * qb:
                    cnt += 1
                mask_cum.append(cnt)

            @blkC.sync
            def _(sync):
                for qb in range(NQB2):
                    if qb >= 2:
                        sync.wait_ge(c_mS2, pairs_end[qb - 2])
                    sync.dma_start(out=QTsb[:, qb % 2, :, :],
                                   in_=qT_r[:, :, qb * QB2:(qb + 1) * QB2]
                                   ).then_inc(c_qtl[qb % 2], 16)
                    for kt in range(2 * (qb + 1)):
                        ev = (qb, kt)
                        i = p_ord[ev]
                        slot = kt_slot[ev]
                        if i >= VD:
                            tq, tk = pairs[i - VD]
                            sync.wait_ge(c_mO, p_ord[(tq, tk)] + 1)
                        sync.dma_start(out=vtr[:, slot, :],
                                       in_=vtk_r[:, kt, :]).then_inc(c_vtf[i % 4], 16)

            @blkC.gpsimd
            def _(g):
                g.dma_start(out=binmT_sb[:], in_=binmT_d.rearrange("m p q -> p m q")
                            ).then_inc(c_vMm, 16)
                g.dma_start(out=ones_sb[:], in_=ones_d).then_inc(c_vMm, 16)
                for qb in range(NQB2):
                    for qi in range(QI):
                        eidx = qb * QI + qi
                        slot = eidx % 2
                        if eidx >= 2:
                            g.wait_ge(c_vEp, eidx - 1)
                        g.dma_start(out=xtl[:, slot, :],
                                    in_=x_d[eidx * 128:(eidx + 1) * 128, :]).then_inc(c_xlf[slot], 16)
                        g.indirect_dma_start(out=ygA[:, slot, :], out_offset=None,
                                             in_=ys_d,
                                             in_offset=IndirectOffsetOnAxis(s1i[:, eidx:eidx + 1], 0)
                                             ).then_inc(c_yga[slot], 16)
                        g.indirect_dma_start(out=ygB[:, slot, :], out_offset=None,
                                             in_=ys_d,
                                             in_offset=IndirectOffsetOnAxis(s2i[:, eidx:eidx + 1], 0)
                                             ).then_inc(c_ygb[slot], 16)
                g.wait_ge(c_ost[0], 16 * ((NT + 1) // 2))
                g.wait_ge(c_ost[1], 16 * (NT // 2))

            @blkC.tensor
            def _(tensor):
                PD = 3

                def emit_S(ev):
                    (qb, kt) = ev
                    p = p_ord[ev]
                    li = kt // 4 if kt < 8 else 2 + (kt - 8) // 8
                    tensor.wait_ge(c_kr[li % 4], 16 * (li // 4 + 1))
                    if p >= 2:
                        tensor.wait_ge(c_sE, p - 1)
                    if kt == 0:
                        tensor.wait_ge(c_qtl[qb % 2], 16 * (qb // 2 + 1))
                    isdiag = (kt >= 2 * qb)
                    for j in range(NJ):
                        mi = tensor.matmul(out=psS[p % 2][:],
                                           lhsT=kres[:, j, kt * 128:(kt + 1) * 128],
                                           rhs=QTsb[:, qb % 2, j, :],
                                           start=(j == 0),
                                           stop=(j == NJ - 1 and not isdiag))
                    if isdiag:
                        mi = tensor.matmul(out=psS[p % 2][:], lhsT=ident_sb[:],
                                           rhs=binmT_sb[:, kt - 2 * qb, :],
                                           start=False, stop=True)
                    mi.then_inc(c_mS2, 1)

                def emit_PV(ev):
                    (qb, kt) = ev
                    p = p_ord[ev]
                    nk = 2 * (qb + 1)
                    tensor.wait_ge(c_sE, p + 1)
                    tensor.wait_ge(c_vtf[kt_sem[ev]], kt_fill[ev])
                    qis = [0, 1] if kt < 2 * qb + 1 else [1]
                    for qi in qis:
                        if kt == 0 and qb >= 1:
                            tensor.wait_ge(c_vEr, 6 * (qb - 1) + (2 if qi == 0 else 4))
                        first = (kt == 0)
                        last = (kt == nk - 1) if qi == 1 else (kt == nk - 2)
                        for dv in range(NDC):
                            tensor.matmul(out=psO[2 * qi + dv][:, :DCW[dv]],
                                          lhsT=PTs[:, p % 4, qi * 128:(qi + 1) * 128],
                                          rhs=vtr[:, kt_slot[ev], dv * 512:dv * 512 + DCW[dv]],
                                          start=first, stop=last)
                        mi = tensor.matmul(out=psL[qi][:, 0:1],
                                           lhsT=PTs[:, p % 4, qi * 128:(qi + 1) * 128],
                                           rhs=ones_sb[:],
                                           start=first, stop=last)
                    mi.then_inc(c_mO, 1)

                tensor.wait_ge(c_vMm, 32)
                for p, ev in enumerate(pairs):
                    emit_S(ev)
                    if p >= PD:
                        emit_PV(pairs[p - PD])
                for p in range(len(pairs) - PD, len(pairs)):
                    emit_PV(pairs[p])

            @blkC.scalar
            def _(scalar):
                scalar.wait_ge(a_ks[0], 16 * (NG // 2))
                scalar.wait_ge(a_ks[1], 16 * (NG // 2))
                kspans = [(0, 512), (512, 512)] + [(1024 * i, 1024)
                                                                   for i in range(1, S // 1024)]
                for li, (off, w) in enumerate(kspans):
                    scalar.dma_start(out=kres[:, :, off:off + w],
                                     in_=kT_r[:, :, off:off + w]
                                     ).then_inc(c_kr[li % 4], 16)
                store_after = {}
                for qb in range(NQB2):
                    for qi in range(QI):
                        eidx = qb * QI + qi
                        trig = (p_ord[(qb + 1, min(3 + qi, 2 * (qb + 1) + 1))]
                                if qb + 1 < NQB2 else None)
                        store_after.setdefault(trig, []).append(eidx)

                def emit_store(eidx):
                    scalar.wait_ge(c_vEp, eidx + 1)
                    scalar.dma_start(out=out_d[eidx * 128:(eidx + 1) * 128, :],
                                     in_=otl[:, eidx % 2, :]).then_inc(c_ost[eidx % 2], 16)

                for (qb, kt) in pairs:
                    p = p_ord[(qb, kt)]
                    scalar.wait_ge(c_mS2, p + 1)
                    if p >= 4:
                        scalar.wait_ge(c_mO, p - 3)
                    scalar.activation(out=PTs[:, p % 4, :], in_=psS[p % 2][:],
                                      func=AF.Exp, bias=fc_sb[:, 1:2]).then_inc(c_sE, 1)
                    for eidx in store_after.get(p, []):
                        emit_store(eidx)
                for eidx in store_after.get(None, []):
                    emit_store(eidx)

            @blkC.vector
            def _(vector):
                for qb in range(NQB2):
                    vector.wait_ge(c_mO, pairs_end[qb])
                    for qi in range(QI):
                        eidx = qb * QI + qi
                        if qb >= 1:
                            vector.wait_ge(c_vEr, 6 * (qb - 1) + (2 if qi == 0 else 4))
                        vector.reciprocal(out=recL[:, qi:qi + 1],
                                          in_=psL[qi][:, 0:1]).then_inc(c_vEr, 1)
                        vector.wait_ge(c_vEr, 6 * qb + (1 if qi == 0 else 3))
                        vector.wait_ge(c_xlf[eidx % 2], 16 * (eidx // 2 + 1))
                        for dv in range(NDC):
                            ins = vector.scalar_tensor_tensor(
                                out=ot1[:, qi, dv * 512:dv * 512 + DCW[dv]],
                                in0=psO[2 * qi + dv][:, :DCW[dv]], scalar=recL[:, qi:qi + 1],
                                in1=xtl[:, eidx % 2, dv * 512:dv * 512 + DCW[dv]],
                                op0=ALU.mult, op1=ALU.add)
                        ins.then_inc(c_vEr, 1)
                    for qi in range(QI):
                        eidx = qb * QI + qi
                        vector.wait_ge(c_vEr, 6 * qb + (2 if qi == 0 else 4))
                        if qi == 0 and qb >= 1:
                            vector.wait_ge(c_vEp, 2 * (qb - 1) + 2)
                        if qi == 1:
                            vector.wait_ge(c_vEp, 2 * qb + 1)
                        vector.wait_ge(c_yga[eidx % 2], 16 * (eidx // 2 + 1))
                        vector.scalar_tensor_tensor(out=ot2[:], in0=ygA[:, eidx % 2, :],
                                                    scalar=cwA[:, eidx:eidx + 1], in1=ot1[:, qi, :],
                                                    op0=ALU.mult, op1=ALU.add).then_inc(c_vEr, 1)
                        vector.wait_ge(c_vEr, 6 * qb + (5 if qi == 0 else 6))
                        vector.wait_ge(c_ygb[eidx % 2], 16 * (eidx // 2 + 1))
                        if eidx >= 2:
                            vector.wait_ge(c_ost[eidx % 2], 16 * (eidx // 2))
                        vector.scalar_tensor_tensor(out=otl[:, eidx % 2, :], in0=ygB[:, eidx % 2, :],
                                                    scalar=cwB[:, eidx:eidx + 1], in1=ot2[:],
                                                    op0=ALU.mult, op1=ALU.add).then_inc(c_vEp, 1)

    return nc




CFG = Cfg()


def _prep_inputs(L0_attn_embeddings, L0_attn_W_Q, L0_attn_W_K, L0_attn_W_V,
                 L0_ffn_Wg, L0_ffn_W1, L0_ffn_W2, L0_ffn_W3, cfg):
    KSUB = cfg.KSUB
    E, C = cfg.E, cfg.CCAP
    w1t = np.ascontiguousarray(np.swapaxes(np.asarray(L0_ffn_W1), 1, 2))
    w3t = np.ascontiguousarray(np.swapaxes(np.asarray(L0_ffn_W3), 1, 2))
    w2t = np.ascontiguousarray(np.swapaxes(np.asarray(L0_ffn_W2), 1, 2))
    wgt = np.ascontiguousarray(np.asarray(L0_ffn_Wg).T)
    ident = np.eye(128, dtype=np.float32).astype(ml_dtypes.bfloat16)
    kk = np.arange(128)[:, None]
    mm = np.arange(128)[None, :]
    uptri = (kk < mm).astype(np.float32).astype(ml_dtypes.bfloat16)
    binm = np.zeros((KSUB, 128, cfg.KW), dtype=np.float32)
    ii = np.arange(128)[:, None]
    jj = np.arange(cfg.KW)[None, :]
    for m in range(KSUB):
        cc = jj // 128
        off = jj % 128
        binm[m] = ((cc < m) | ((cc == m) & (off <= ii))).astype(np.float32)
    binm = binm.astype(ml_dtypes.bfloat16)
    kk2 = np.arange(128)[:, None]
    qq2 = np.arange(256)[None, :]
    binmT = np.stack([-50.0 * (qq2 < kk2 + 128 * m) for m in range(2)]).astype(np.float32)
    binmT = binmT.astype(ml_dtypes.bfloat16)
    ee = np.arange(E, dtype=np.float32)
    econst = np.zeros((128, 4, E), dtype=np.float32)
    econst[:, 0, :] = ee * C
    econst[:, 1, :] = (ee + 1) * C
    econst[:, 2, :] = E * C
    econst[:, 3, :] = 1e9
    common = {
        "wq": np.ascontiguousarray(np.asarray(L0_attn_W_Q)),
        "wk": np.ascontiguousarray(np.asarray(L0_attn_W_K)),
        "wv": np.ascontiguousarray(np.asarray(L0_attn_W_V)),
        "wgt": wgt, "w1t": w1t, "w3t": w3t, "w2t": w2t,
        "identb": ident, "uptri": uptri, "binm": binm, "econst": econst,
        "binmT": binmT, "ones1": np.ones((128, 1), np.float32).astype(ml_dtypes.bfloat16),
        "fconst": np.tile(np.array([[-0.5, -16.0]], dtype=np.float32), (128, 1)),
    }
    emb = np.asarray(L0_attn_embeddings)
    in_maps = []
    for b in range(emb.shape[0]):
        m = dict(common)
        m["x"] = np.ascontiguousarray(emb[b])
        in_maps.append(m)
    return in_maps


def kernel(L0_attn_embeddings, L0_attn_W_Q, L0_attn_W_K, L0_attn_W_V,
           L0_ffn_Wg, L0_ffn_W1, L0_ffn_W2, L0_ffn_W3):
    cfg = CFG
    in_maps = _prep_inputs(L0_attn_embeddings, L0_attn_W_Q, L0_attn_W_K,
                           L0_attn_W_V, L0_ffn_Wg, L0_ffn_W1, L0_ffn_W2,
                           L0_ffn_W3, cfg)
    nc = build_program(cfg)
    res = run_bass_kernel_spmd(nc, in_maps, core_ids=list(range(len(in_maps))))
    out = np.stack([np.asarray(r["out"]) for r in res.results], axis=0)
    return out.astype(np.float32)


# version 43
# speedup vs baseline: 1.0598x; 1.0598x over previous
import numpy as np
import ml_dtypes
from contextlib import ExitStack

import concourse.bass as bass
import concourse.mybir as mybir
from concourse.bass_core_rust import IndirectOffsetOnAxis
from concourse.bass_utils import run_bass_kernel_spmd

F32 = mybir.dt.float32
BF16 = mybir.dt.bfloat16
I32 = mybir.dt.int32
AF = mybir.ActivationFunctionType
ALU = mybir.AluOpType
AX = mybir.AxisListType


class Cfg:
    def __init__(self, S=8192, D=1024, H=2816, E=8, CH=1024, QB=1024, KW=512,
                 CCAP=2432):
        self.S, self.D, self.H, self.E = S, D, H, E
        self.CH = CH
        self.QB = QB
        self.KW = KW
        self.CCAP = CCAP
        self.NJ = D // 128
        self.HC = H // 128
        self.NT = S // 128
        self.NG = S // 512
        self.NTB = (CH + 511) // 512
        self.TBW = [min(512, CH - 512 * i) for i in range(self.NTB)]
        self.T128 = CH // 128
        self.NDC = (D + 511) // 512
        self.DCW = [min(512, D - 512 * i) for i in range(self.NDC)]
        self.NQB = S // QB
        self.KSUB = KW // 128
        self.QR = QB // 128
        self.slabs = [(h, min(h + 2, self.HC)) for h in range(0, self.HC, 2)]
        self.NSLAB = len(self.slabs)
        self.SLOTS = E * CCAP
        self.SLOTS_PAD = self.SLOTS + 128
        widths = []
        r = CCAP
        while r > 0:
            w = min(CH, r)
            assert w % 128 == 0
            widths.append(w)
            r -= w
        self.NSUB = len(widths)
        self.NCHS = E * self.NSUB
        w0 = [widths[-1]] + widths[:-1]
        self.chunks = [(e, sub, w) for e in range(E)
                       for sub, w in enumerate(w0 if e == 0 else widths)]

    def KT(self, qb):
        return (qb + 1) * self.QB // self.KW


def build_program(cfg):
    S, D, H, E = cfg.S, cfg.D, cfg.H, cfg.E
    NJ, HC, NT, NG = cfg.NJ, cfg.HC, cfg.NT, cfg.NG
    CH, NTB, TBW, T128 = cfg.CH, cfg.NTB, cfg.TBW, cfg.T128
    NDC, DCW = cfg.NDC, cfg.DCW
    QB, KW, NQB, KSUB, QR = cfg.QB, cfg.KW, cfg.NQB, cfg.KSUB, cfg.QR
    NSLAB, CCAP, SLOTS, SLOTS_PAD = cfg.NSLAB, cfg.CCAP, cfg.SLOTS, cfg.SLOTS_PAD
    NCHS, NSUB = cfg.NCHS, cfg.NSUB

    nc = bass.Bass("TRN2", target_bir_lowering=False)

    x_d = nc.dram_tensor("x", [S, D], F32, kind="ExternalInput").ap()
    wq_d = nc.dram_tensor("wq", [D, D], F32, kind="ExternalInput").ap()
    wk_d = nc.dram_tensor("wk", [D, D], F32, kind="ExternalInput").ap()
    wv_d = nc.dram_tensor("wv", [D, D], F32, kind="ExternalInput").ap()
    wgt_d = nc.dram_tensor("wgt", [D, E], F32, kind="ExternalInput").ap()
    w1t_d = nc.dram_tensor("w1t", [E, D, H], F32, kind="ExternalInput").ap()
    w3t_d = nc.dram_tensor("w3t", [E, D, H], F32, kind="ExternalInput").ap()
    w2t_d = nc.dram_tensor("w2t", [E, H, D], F32, kind="ExternalInput").ap()
    w1b_d = nc.dram_tensor("w1b_s", [E, D, H], BF16, kind="Internal").ap()
    w3b_d = nc.dram_tensor("w3b_s", [E, D, H], BF16, kind="Internal").ap()
    w2b_d = nc.dram_tensor("w2b_s", [E, H, D], BF16, kind="Internal").ap()
    identb_d = nc.dram_tensor("identb", [128, 128], BF16, kind="ExternalInput").ap()
    uptri_d = nc.dram_tensor("uptri", [128, 128], BF16, kind="ExternalInput").ap()
    fconst_d = nc.dram_tensor("fconst", [128, 2], F32, kind="ExternalInput").ap()
    econst_d = nc.dram_tensor("econst", [128, 4, E], F32, kind="ExternalInput").ap()
    binm_d = nc.dram_tensor("binm", [KSUB, 128, KW], BF16, kind="ExternalInput").ap()
    binmT_d = nc.dram_tensor("binmT", [2, 128, 256], BF16, kind="ExternalInput").ap()
    ones_d = nc.dram_tensor("ones1", [128, 1], BF16, kind="ExternalInput").ap()
    out_d = nc.dram_tensor("out", [S, D], F32, kind="ExternalOutput").ap()

    xn_d = nc.dram_tensor("xn_s", [S, D], BF16, kind="Internal").ap()
    qT_d = nc.dram_tensor("qT_s", [D, S], BF16, kind="Internal").ap()
    kT_d = nc.dram_tensor("kT_s", [D, S], BF16, kind="Internal").ap()
    vtk_d = nc.dram_tensor("vtk_s", [S, D], BF16, kind="Internal").ap()
    xg_d = nc.dram_tensor("xg_s", [SLOTS_PAD, D], BF16, kind="Internal").ap()
    ys_d = nc.dram_tensor("ys_s", [SLOTS_PAD, D], F32, kind="Internal").ap()

    qT_r = qT_d.rearrange("(j p) t -> p j t", p=128)
    kT_r = kT_d.rearrange("(j p) t -> p j t", p=128)
    vtk_r = vtk_d.rearrange("(i p) d -> p i d", p=128)
    xg_r = xg_d.rearrange("(n p) d -> p n d", p=128)
    ys_r = ys_d.rearrange("(n p) d -> p n d", p=128)

    with ExitStack() as top:
        TE = top.enter_context
        ident_sb = TE(nc.sbuf_tensor("ident_sb", [128, 128], BF16))
        uptri_sb = TE(nc.sbuf_tensor("uptri_sb", [128, 128], BF16))
        fc_sb = TE(nc.sbuf_tensor("fc_sb", [128, 2], F32))
        econ_sb = TE(nc.sbuf_tensor("econ_sb", [128, 4, E], F32))
        binm_sb = TE(nc.sbuf_tensor("binm_sb", [128, KSUB, KW], BF16))
        cw_sb = TE(nc.sbuf_tensor("cw_sb", [128, NT, E], F32))
        s1f = TE(nc.sbuf_tensor("s1f", [128, NT], F32))
        s2f = TE(nc.sbuf_tensor("s2f", [128, NT], F32))
        s1i = TE(nc.sbuf_tensor("s1i", [128, NT], I32))
        s2i = TE(nc.sbuf_tensor("s2i", [128, NT], I32))
        cwA = TE(nc.sbuf_tensor("cwA", [128, NT], F32))
        cwB = TE(nc.sbuf_tensor("cwB", [128, NT], F32))
        rtb = TE(nc.sbuf_tensor("rtb", [128, 1], F32))
        pab = ExitStack()
        tstage = pab.enter_context(nc.sbuf_tensor("tstage", [128, 4, D], BF16))

        def sem(name):
            return TE(nc.semaphore(name))

        a_xl = [sem(f"a_xl{i}") for i in range(8)]
        a_vx = sem("a_vx")
        a_sq = sem("a_sq")
        a_vr = sem("a_vr")
        a_vc = sem("a_vc")
        a_sc = sem("a_sc")
        a_sr = sem("a_sr")
        a_tp = sem("a_tp")
        a_se = sem("a_se")
        a_mq = sem("a_mq")
        a_ve = sem("a_ve")
        a_mk = sem("a_mk")
        a_ke = sem("a_ke")
        a_mv = sem("a_mv")
        a_vse = sem("a_vse")
        a_mg = sem("a_mg")
        a_ge = sem("a_ge")
        a_vg = sem("a_vg")
        a_ld = sem("a_ld")
        awz = [sem(f"awz{i}") for i in range(4)]
        awm = sem("awm")
        a_xn = [sem(f"a_xn{i}") for i in range(4)]
        a_qs = [sem(f"a_qs{i}") for i in range(2)]
        a_ks = [sem(f"a_ks{i}") for i in range(2)]
        a_vs = [sem(f"a_vs{i}") for i in range(2)]
        b_rt = sem("b_rt")
        b_rv = sem("b_rv")
        b_rm = sem("b_rm")
        b_xns = [sem(f"b_xns{i}") for i in range(4)]
        b_sc = [sem(f"b_sc{i}") for i in range(4)]
        b_xgl = [sem(f"b_xgl{i}") for i in range(2)]
        b_mT2 = sem("b_mT2")
        b_se2 = sem("b_se2")
        b_w1f = [sem(f"b_w1f{i}") for i in range(3)]
        b_w3f = [sem(f"b_w3f{i}") for i in range(3)]
        b_w2f = sem("b_w2f")
        b_mG = sem("b_mG")
        b_sS = sem("b_sS")
        b_vH = sem("b_vH")
        b_mW = sem("b_mW")
        b_vY = sem("b_vY")
        b_yst = sem("b_yst")
        c_qtl = [sem(f"c_qtl{i}") for i in range(2)]
        c_vtf = [sem(f"c_vtf{i}") for i in range(4)]
        c_mS2 = sem("c_mS2")
        c_sE = sem("c_sE")
        c_vMm = sem("c_vMm")
        c_mO = sem("c_mO")
        c_vEr = sem("c_vEr")
        c_vEp = sem("c_vEp")
        c_xlf = [sem(f"c_xlf{i}") for i in range(2)]
        c_yga = [sem(f"c_yga{i}") for i in range(2)]
        c_ygb = [sem(f"c_ygb{i}") for i in range(2)]
        c_ost = [sem(f"c_ost{i}") for i in range(2)]
        c_kr = [sem(f"c_kr{i}") for i in range(4)]
        b_cast = [sem(f"b_cast{i}") for i in range(min(E, 4))]

        with ExitStack() as pa:
            PE = pa.enter_context
            xb = PE(nc.sbuf_tensor("xb", [128, 8, D], F32))
            sdump = PE(nc.sbuf_tensor("sdump", [128, 2, D], F32))
            ssum = PE(nc.sbuf_tensor("ssum", [128, 4], F32))
            ssq = PE(nc.sbuf_tensor("ssq", [128, 4], F32))
            mean4 = PE(nc.sbuf_tensor("mean4", [128, 4], F32))
            msq4 = PE(nc.sbuf_tensor("msq4", [128, 4], F32))
            m24 = PE(nc.sbuf_tensor("m24", [128, 4], F32))
            var4 = PE(nc.sbuf_tensor("var4", [128, 4], F32))
            ln4 = PE(nc.sbuf_tensor("ln4", [128, 4], F32))
            r4 = PE(nc.sbuf_tensor("r4", [128, 4], F32))
            negm4 = PE(nc.sbuf_tensor("negm4", [128, 4], F32))
            xnt = PE(nc.sbuf_tensor("xnt", [128, 8, D], BF16))
            xnTst = PE(nc.sbuf_tensor("xnTst", [128, 2, NJ, 512], BF16))
            QTst = PE(nc.sbuf_tensor("QTst", [128, 2, NJ, 512], BF16))
            KTst = PE(nc.sbuf_tensor("KTst", [128, 2, NJ, 512], BF16))
            Vst = PE(nc.sbuf_tensor("Vst", [128, 2, 4, D], BF16))
            WQsb = PE(nc.sbuf_tensor("WQsb", [128, NJ, D], BF16))
            WKsb = PE(nc.sbuf_tensor("WKsb", [128, NJ, D], BF16))
            WVsb = PE(nc.sbuf_tensor("WVsb", [128, NJ, D], BF16))
            WGsb = PE(nc.sbuf_tensor("WGsb", [128, NJ, E], BF16))
            zsb = PE(nc.sbuf_tensor("zsb", [128, 8192], BF16))
            selb4 = PE(nc.sbuf_tensor("selb4", [128, 4, E], F32))
            selbi4 = PE(nc.sbuf_tensor("selbi4", [128, 2, 4, E], I32))
            totb = PE(nc.sbuf_tensor("totb", [128, 4, E], BF16))
            selbE4 = PE(nc.sbuf_tensor("selbE4", [128, E, 4], F32))
            segm4 = PE(nc.sbuf_tensor("segm4", [128, E, 4], F32))
            rcs4 = PE(nc.sbuf_tensor("rcs4", [128, 2, E, 4], F32))
            offv4 = PE(nc.sbuf_tensor("offv4", [128, 4, E], F32))
            offc4 = PE(nc.sbuf_tensor("offc4", [128, 4, E], F32))
            selsl4 = PE(nc.sbuf_tensor("selsl4", [128, 4, E], F32))
            selsl24 = PE(nc.sbuf_tensor("selsl24", [128, 4, E], F32))
            eqm4 = PE(nc.sbuf_tensor("eqm4", [128, 4, E], F32))
            teq4 = PE(nc.sbuf_tensor("teq4", [128, 4, E], F32))
            ovmi4 = PE(nc.sbuf_tensor("ovmi4", [128, 4, E], I32))
            s1f4 = PE(nc.sbuf_tensor("s1f4", [128, 4, 1], F32))
            s2f4 = PE(nc.sbuf_tensor("s2f4", [128, 4, 1], F32))
            baseg = PE(nc.sbuf_tensor("baseg", [128, 1, E], F32))
            carry = PE(nc.sbuf_tensor("carry", [128, E], F32))
            etr4 = PE(nc.sbuf_tensor("etr4", [128, 4, E], F32))
            ge4 = PE(nc.sbuf_tensor("ge4", [128, 4, E], F32))
            sm4 = PE(nc.sbuf_tensor("sm4", [128, 4, E], F32))
            gge4 = PE(nc.sbuf_tensor("gge4", [128, 4, E], F32))
            gt4 = PE(nc.sbuf_tensor("gt4", [128, 4, E], F32))
            gmk4 = PE(nc.sbuf_tensor("gmk4", [128, 4, E], F32))
            gs4 = PE(nc.sbuf_tensor("gs4", [128, 4, 1], F32))
            grs4 = PE(nc.sbuf_tensor("grs4", [128, 4, 1], F32))
            gm14 = PE(nc.sbuf_tensor("gm14", [128, 4, 1], F32))
            gm24 = PE(nc.sbuf_tensor("gm24", [128, 4, 1], F32))
            gden4 = PE(nc.sbuf_tensor("gden4", [128, 4, 1], F32))
            grd4 = PE(nc.sbuf_tensor("grd4", [128, 4, 1], F32))
            psT = [PE(nc.psum_tensor(f"psT{i}", [128, 128], BF16)) for i in range(3)]
            psQK = [PE(nc.psum_tensor(f"psQK{i}", [128, 512], F32)) for i in range(2)]
            psV = [PE(nc.psum_tensor(f"psV{i}", [128, 512], F32)) for i in range(2)]
            psGt = PE(nc.psum_tensor("psG", [128, 2, 6, E], F32))
            psG = [psGt[:, 0], psGt[:, 1]]
            blkA = PE(nc.Block())

            NVC, VCW = NDC, DCW
            QSC = 1.0 / float(np.sqrt(D))

            def psqk_prior_wait(eng, g, phase, jq):
                u = 2 * NJ * g + (NJ if phase == "k" else 0) + jq
                if u < 2:
                    return
                up = u - 2
                gp, rem = divmod(up, 2 * NJ)
                if rem < NJ:
                    eng.wait_ge(a_ve, NJ * gp + rem + 1)
                else:
                    eng.wait_ge(a_ke, NJ * gp + (rem - NJ) + 1)

            KP = 4
            cast_pieces = {}
            for e_ in range(E):
                pl = []
                for i_ in range(KP):
                    h0, h1 = (H * i_) // KP, (H * (i_ + 1)) // KP
                    pl.append((w1b_d[e_][:, h0:h1], w1t_d[e_][:, h0:h1]))
                    pl.append((w3b_d[e_][:, h0:h1], w3t_d[e_][:, h0:h1]))
                    pl.append((w2b_d[e_][h0:h1, :], w2t_d[e_][h0:h1, :]))
                cast_pieces[e_] = pl

            @blkA.gpsimd
            def _(g):
                g.dma_start(out=ident_sb[:], in_=identb_d).then_inc(a_ld, 16)
                g.dma_start(out=uptri_sb[:], in_=uptri_d).then_inc(a_ld, 16)
                g.dma_start(out=fc_sb[:], in_=fconst_d).then_inc(a_ld, 16)
                g.dma_start(out=econ_sb[:], in_=econst_d).then_inc(a_ld, 16)
                g.dma_start(out=binm_sb[:], in_=binm_d.rearrange("m p k -> p m k")).then_inc(a_ld, 16)
                g.dma_start(out=WQsb[:], in_=wq_d.rearrange("(j p) q -> p j q", p=128)).then_inc(a_ld, 16)
                g.dma_start(out=WKsb[:], in_=wk_d.rearrange("(j p) q -> p j q", p=128)).then_inc(a_ld, 16)
                g.dma_start(out=WVsb[:], in_=wv_d.rearrange("(j p) q -> p j q", p=128)).then_inc(a_ld, 16)
                g.dma_start(out=WGsb[:], in_=wgt_d.rearrange("(j p) e -> p j e", p=128)).then_inc(a_ld, 16)
                g.memset(zsb[:], 0.0).then_inc(awm, 1)
                g.wait_ge(awm, 1)
                nrows = SLOTS_PAD // 128
                zw = 8192 // D
                nz = 0
                for n0 in range(0, nrows, zw):
                    w = min(zw, nrows - n0)
                    if nz >= 4:
                        g.wait_ge(awz[nz % 4], 16 * (nz // 4))
                    g.dma_start(out=xg_r[:, n0:n0 + w, :],
                                in_=zsb[:, :w * D]).then_inc(awz[nz % 4], 16)
                    nz += 1
                if nz >= 4:
                    g.wait_ge(awz[nz % 4], 16 * (nz // 4))
                g.dma_start(out=ys_r[:, SLOTS // 128:, :],
                            in_=zsb[:, :D]).then_inc(awz[nz % 4], 16)
                nz += 1
                for k4 in range(4):
                    cnt = (nz - k4 + 3) // 4
                    if cnt:
                        g.wait_ge(awz[k4], 16 * cnt)
                apieces = [(e_, oi) for e_ in range(min(2, E))
                           for oi in cast_pieces[e_]]
                nap = [0]
                for i in range(NT):
                    g.wait_ge(b_rt, i // 4 + 1)
                    g.wait_ge(b_xns[i % 4], 16 * (i // 4 + 1))
                    g.indirect_dma_start(out=xg_d, out_offset=IndirectOffsetOnAxis(s1i[:, i:i + 1], 0),
                                         in_=tstage[:, i % 4, :],
                                         in_offset=None).then_inc(b_sc[i % 4], 16)
                    g.indirect_dma_start(out=xg_d, out_offset=IndirectOffsetOnAxis(s2i[:, i:i + 1], 0),
                                         in_=tstage[:, i % 4, :],
                                         in_offset=None).then_inc(b_sc[i % 4], 16)
                    if i % 4 == 3 and nap[0] < len(apieces):
                        take = 2 if (i // 4) % 2 == 0 else 1
                        for _ in range(take):
                            if nap[0] < len(apieces):
                                e_, (o_, i_) = apieces[nap[0]]
                                g.dma_start(out=o_, in_=i_).then_inc(b_cast[e_ % 4], 16)
                                nap[0] += 1
                while nap[0] < len(apieces):
                    e_, (o_, i_) = apieces[nap[0]]
                    g.dma_start(out=o_, in_=i_).then_inc(b_cast[e_ % 4], 16)
                    nap[0] += 1

            @blkA.sync
            def _(sync):
                nst = [0]

                def stage_one(i):
                    sync.wait_ge(a_xn[i % 4], 16 * (i // 4 + 1))
                    if i >= 4:
                        sync.wait_ge(b_sc[i % 4], 32 * (i // 4))
                    sync.dma_start(out=tstage[:, i % 4, :],
                                   in_=xn_d[i * 128:(i + 1) * 128, :]).then_inc(b_xns[i % 4], 16)

                def stage_to(lim):
                    while nst[0] <= min(lim, NT - 1):
                        stage_one(nst[0])
                        nst[0] += 1

                def stores(g):
                    sl = g % 2
                    sync.wait_ge(a_ve, NJ * (g + 1))
                    sync.dma_start(out=qT_r[:, :, g * 512:(g + 1) * 512],
                                   in_=QTst[:, sl, :, :]).then_inc(a_qs[sl], 16)
                    sync.wait_ge(a_ke, NJ * (g + 1))
                    sync.dma_start(out=kT_r[:, :, g * 512:(g + 1) * 512],
                                   in_=KTst[:, sl, :, :]).then_inc(a_ks[sl], 16)
                    sync.wait_ge(a_vse, 4 * NVC * (g + 1))
                    sync.dma_start(out=vtk_r[:, g * 4:(g + 1) * 4, :],
                                   in_=Vst[:, sl, :, :]).then_inc(a_vs[sl], 16)

                for g in range(NG):
                    for i in range(4):
                        t = 4 * g + i
                        slot = t % 8
                        if t >= 8:
                            sync.wait_ge(a_vx, t - 7)
                            sync.wait_ge(a_sq, t - 7)
                        sync.dma_start(out=xb[:, slot, :],
                                       in_=x_d[t * 128:(t + 1) * 128, :]).then_inc(a_xl[slot], 16)
                    for u in range(max(0, 4 * g - 2), 4 * g + 2):
                        sync.wait_ge(a_vx, u + 1)
                        if u >= 4:
                            sync.wait_ge(a_xn[u % 4], 16 * (u // 4))
                        sync.dma_start(out=xn_d[u * 128:(u + 1) * 128, :],
                                       in_=xnt[:, u % 8, :]).then_inc(a_xn[u % 4], 16)
                        stage_to(u - 12)
                    if g >= 1:
                        stores(g - 1)
                for u in (NT - 2, NT - 1):
                    sync.wait_ge(a_vx, u + 1)
                    if u >= 4:
                        sync.wait_ge(a_xn[u % 4], 16 * (u // 4))
                    sync.dma_start(out=xn_d[u * 128:(u + 1) * 128, :],
                                   in_=xnt[:, u % 8, :]).then_inc(a_xn[u % 4], 16)
                    stage_to(u - 12)
                stores(NG - 1)
                stage_to(NT - 1)
                for s4 in range(4):
                    sync.wait_ge(a_xn[s4], 16 * (NT // 4))
                sync.wait_ge(a_qs[0], 16 * ((NG + 1) // 2))
                sync.wait_ge(a_qs[1], 16 * (NG // 2))
                sync.wait_ge(a_ks[0], 16 * ((NG + 1) // 2))
                sync.wait_ge(a_ks[1], 16 * (NG // 2))
                sync.wait_ge(a_vs[0], 16 * ((NG + 1) // 2))
                sync.wait_ge(a_vs[1], 16 * (NG // 2))

            r1done, r2done = {}, {}

            @blkA.vector
            def _(vector):
                def qevacs(g):
                    for jq in range(NJ):
                        if jq == 0 and g >= 2:
                            vector.wait_ge(a_qs[g % 2], 16 * (g // 2))
                        vector.wait_ge(a_mq, NJ * g + jq + 1)
                        vector.tensor_scalar_mul(out=QTst[:, g % 2, jq, :], in0=psQK[jq % 2][:],
                                                 scalar1=QSC).then_inc(a_ve, 1)

                def gating(g):
                    base = 12 * g

                    def gw(ins):
                        ins.then_inc(a_vg, 1)

                    def gwait(k):
                        vector.wait_ge(a_vg, base + k)

                    vector.wait_ge(a_ge, g + 1)
                    gw(vector.tensor_reduce(out=gs4[:], in_=ge4[:], axis=AX.X, op=ALU.add))
                    gwait(1)
                    gw(vector.reciprocal(out=grs4[:], in_=gs4[:]))
                    gwait(2)
                    gw(vector.tensor_tensor(out=sm4[:], in0=ge4[:],
                                            in1=grs4[:].broadcast_to([128, 4, E]), op=ALU.mult))
                    gwait(3)
                    gw(vector.tensor_reduce(out=gm14[:], in_=sm4[:], axis=AX.X, op=ALU.max))
                    gwait(4)
                    gw(vector.tensor_tensor(out=gge4[:], in0=sm4[:],
                                            in1=gm14[:].broadcast_to([128, 4, E]), op=ALU.is_ge))
                    gwait(5)
                    gw(vector.scalar_tensor_tensor(out=gt4[:], in0=gge4[:], scalar=-1e30,
                                                   in1=sm4[:], op0=ALU.mult, op1=ALU.add))
                    gwait(6)
                    gw(vector.tensor_reduce(out=gm24[:], in_=gt4[:], axis=AX.X, op=ALU.max))
                    gwait(7)
                    gw(vector.scalar_tensor_tensor(out=gden4[:], in0=gm14[:], scalar=1e-20,
                                                   in1=gm24[:], op0=ALU.add, op1=ALU.add))
                    gwait(8)
                    gw(vector.reciprocal(out=grd4[:], in_=gden4[:]))
                    gwait(7)
                    gw(vector.tensor_tensor(out=gmk4[:], in0=sm4[:],
                                            in1=gm24[:].broadcast_to([128, 4, E]), op=ALU.is_ge))
                    gwait(10)
                    gw(vector.tensor_mul(out=gt4[:], in0=sm4[:], in1=gmk4[:]))
                    gwait(11)
                    gw(vector.tensor_tensor(out=cw_sb[:, 4 * g:4 * g + 4, :], in0=gt4[:],
                                            in1=grd4[:].broadcast_to([128, 4, E]), op=ALU.mult))

                rv = [0]

                def rop(ins):
                    rv[0] += 1
                    ins.then_inc(b_rv, 1)

                def rwait():
                    vector.wait_ge(b_rv, rv[0])

                def routing1(g):
                    sl = g % 2
                    vector.wait_ge(a_vg, 12 * (g + 1))
                    rop(vector.tensor_scalar(out=selb4[:], in0=cw_sb[:, 4 * g:4 * g + 4, :],
                                             scalar1=0.0, scalar2=None, op0=ALU.is_gt))
                    rwait()
                    rop(vector.tensor_copy(out=selbi4[:, sl], in_=selb4[:]))
                    rop(vector.tensor_copy(out=selbE4[:].rearrange("p e t -> p t e"),
                                           in_=selb4[:]))
                    rwait()
                    rop(vector.tensor_tensor_scan(out=rcs4[:, sl].rearrange("p e t -> p (e t)"),
                                                  data0=segm4[:].rearrange("p e t -> p (e t)"),
                                                  data1=selbE4[:].rearrange("p e t -> p (e t)"),
                                                  initial=0.0, op0=ALU.mult, op1=ALU.add))
                    rwait()
                    if g >= 4:
                        vector.wait_ge(b_rm, g - 3)
                    rop(vector.tensor_copy(out=totb[:, g % 4], in_=rcs4[:, sl, :, 3]))
                    r1done[g] = rv[0]

                def routing2(gg):
                    sl = gg % 2
                    vector.wait_ge(b_rm, gg + 1)
                    rop(vector.tensor_tensor(out=baseg[:, 0, :], in0=psG[gg % 2][:, 4, :],
                                             in1=carry[:], op=ALU.add))
                    rwait()
                    rop(vector.scalar_tensor_tensor(out=offv4[:],
                                                    in0=rcs4[:, sl].rearrange("p e t -> p t e"),
                                                    scalar=-1.0,
                                                    in1=baseg[:].broadcast_to([128, 4, E]),
                                                    op0=ALU.add, op1=ALU.add))
                    rop(vector.tensor_tensor(out=carry[:], in0=carry[:], in1=psG[gg % 2][:, 5, :],
                                             op=ALU.add))
                    rwait()
                    rop(vector.tensor_tensor(out=ovmi4[:], in0=offv4[:],
                                             in1=econ_sb[:, 1:2, :].broadcast_to([128, 4, E]),
                                             op=ALU.is_ge))
                    rop(vector.tensor_copy(out=offc4[:], in_=offv4[:]))
                    rwait()
                    rop(vector.copy_predicated(out=offc4[:], mask=ovmi4[:], data=etr4[:]))
                    rop(vector.memset(selsl4[:], 1e9))
                    rwait()
                    rop(vector.copy_predicated(out=selsl4[:], mask=selbi4[:, sl], data=offc4[:]))
                    rwait()
                    rop(vector.tensor_reduce(out=s1f4[:], in_=selsl4[:], axis=AX.X, op=ALU.min))
                    rwait()
                    rop(vector.tensor_tensor(out=eqm4[:], in0=selsl4[:],
                                             in1=s1f4[:].broadcast_to([128, 4, E]), op=ALU.is_le))
                    rwait()
                    rop(vector.tensor_mul(out=teq4[:], in0=cw_sb[:, 4 * gg:4 * gg + 4, :],
                                          in1=eqm4[:]))
                    rop(vector.scalar_tensor_tensor(out=selsl24[:], in0=eqm4[:], scalar=1e9,
                                                    in1=selsl4[:], op0=ALU.mult, op1=ALU.add))
                    rwait()
                    rop(vector.tensor_reduce(out=cwA[:, 4 * gg:4 * gg + 4], in_=teq4[:],
                                             axis=AX.X, op=ALU.add))
                    rop(vector.tensor_reduce(out=s2f4[:], in_=selsl24[:], axis=AX.X, op=ALU.min))
                    rwait()
                    rop(vector.tensor_tensor(out=eqm4[:], in0=selsl24[:],
                                             in1=s2f4[:].broadcast_to([128, 4, E]), op=ALU.is_le))
                    rwait()
                    rop(vector.tensor_mul(out=teq4[:], in0=cw_sb[:, 4 * gg:4 * gg + 4, :],
                                          in1=eqm4[:]))
                    rwait()
                    rop(vector.tensor_reduce(out=cwB[:, 4 * gg:4 * gg + 4], in_=teq4[:],
                                             axis=AX.X, op=ALU.add))
                    rop(vector.tensor_copy(out=s1i[:, 4 * gg:4 * gg + 4], in_=s1f4[:, :, 0]))
                    rop(vector.tensor_copy(out=s2i[:, 4 * gg:4 * gg + 4], in_=s2f4[:, :, 0]))
                    rwait()
                    vector.memset(rtb[:], 0.0).then_inc(b_rt, 1)
                    r2done[gg] = rv[0]

                vector.wait_ge(a_ld, 16 * 9)
                rop(vector.tensor_copy(out=carry[:], in_=econ_sb[:, 0, :]))
                rop(vector.memset(segm4[:], 1.0))
                rop(vector.memset(etr4[:], float(SLOTS)))
                rwait()
                rop(vector.memset(segm4[:, :, 0:1], 0.0))
                rwait()

                def lnchain(tr):
                    for i in range(4):
                        t = 4 * tr + i
                        slot = t % 8
                        vector.wait_ge(a_xl[slot], 16 * (t // 8 + 1))
                        if i == 0 and tr >= 1:
                            vector.wait_ge(a_vx, 4 * tr)
                            vector.wait_ge(a_sc, tr)
                        ins = vector.reduce_sum(out=ssum[:, i:i + 1], in_=xb[:, slot, :], axis=AX.X)
                        if i == 3:
                            ins.then_inc(a_vr, 1)
                    vector.wait_ge(a_vr, tr + 1)
                    vector.wait_ge(a_sq, 4 * (tr + 1))
                    vector.tensor_scalar_mul(out=mean4[:], in0=ssum[:], scalar1=1.0 / D).then_inc(a_vc, 1)
                    vector.tensor_scalar_mul(out=msq4[:], in0=ssq[:], scalar1=1.0 / D)
                    vector.wait_ge(a_vc, 4 * tr + 1)
                    vector.tensor_mul(out=m24[:], in0=mean4[:], in1=mean4[:]).then_inc(a_vc, 1)
                    vector.wait_ge(a_vc, 4 * tr + 2)
                    vector.scalar_tensor_tensor(out=var4[:], in0=msq4[:], scalar=1e-5, in1=m24[:],
                                                op0=ALU.add, op1=ALU.subtract).then_inc(a_vc, 1)
                    vector.wait_ge(a_sr, tr + 1)
                    vector.scalar_tensor_tensor(out=negm4[:], in0=mean4[:], scalar=-1.0, in1=r4[:],
                                                op0=ALU.mult, op1=ALU.mult).then_inc(a_vc, 1)
                    vector.wait_ge(a_vc, 4 * tr + 4)
                    for i in range(4):
                        t = 4 * tr + i
                        slot = t % 8
                        if t >= 8:
                            vector.wait_ge(a_tp, NJ * (t - 7))
                            vector.wait_ge(a_xn[t % 4], 16 * ((t - 4) // 4))
                        vector.tensor_scalar(out=xnt[:, slot, :], in0=xb[:, slot, :],
                                             scalar1=r4[:, i:i + 1], scalar2=negm4[:, i:i + 1],
                                             op0=ALU.mult, op1=ALU.add).then_inc(a_vx, 1)

                lnchain(0)
                for g in range(NG):
                    if g >= 1:
                        gating(g - 1)
                    qevacs(g)
                    if g + 1 < NG:
                        lnchain(g + 1)
                    if g >= 3:
                        routing2(g - 3)
                    if g >= 1:
                        routing1(g - 1)
                gating(NG - 1)
                routing2(NG - 3)
                routing1(NG - 1)
                routing2(NG - 2)
                routing2(NG - 1)
                pass

            @blkA.scalar
            def _(scalar):
                def gexps(g):
                    scalar.wait_ge(a_mg, g + 1)
                    if g >= 1:
                        scalar.wait_ge(a_vg, 12 * (g - 1) + 3)
                    scalar.activation(out=ge4[:], in_=psG[g % 2][:, 0:4, :],
                                      func=AF.Exp).then_inc(a_ge, 1)

                def sqln(tr):
                    for i in range(4):
                        t = 4 * tr + i
                        slot = t % 8
                        scalar.wait_ge(a_xl[slot], 16 * (t // 8 + 1))
                        if t >= 2:
                            scalar.wait_ge(a_sq, t - 1)
                        if i == 0 and tr >= 1:
                            scalar.wait_ge(a_vx, 4 * tr)
                        scalar.activation(out=sdump[:, t % 2, :], in_=xb[:, slot, :], func=AF.Square,
                                          accum_out=ssq[:, i:i + 1]).then_inc(a_sq, 1)
                    scalar.wait_ge(a_vc, 4 * tr + 3)
                    scalar.activation(out=ln4[:], in_=var4[:], func=AF.Ln).then_inc(a_sc, 1)
                    scalar.wait_ge(a_sc, tr + 1)
                    if tr >= 1:
                        scalar.wait_ge(a_vx, 4 * tr)
                    scalar.activation(out=r4[:], in_=ln4[:], func=AF.Exp, scale=fc_sb[:, 0:1]).then_inc(a_sr, 1)

                for g in range(NG):
                    sl = g % 2
                    if g == 0:
                        scalar.wait_ge(a_ld, 16 * 9)
                        sqln(0)
                    if g >= 1:
                        gexps(g - 1)
                    for i in range(4):
                        t = 4 * g + i
                        for j in range(NJ):
                            idx = t * NJ + j
                            if i == 0 and j == 0 and g >= 2:
                                scalar.wait_ge(a_mg, g - 1)
                            scalar.wait_ge(a_tp, idx + 1)
                            scalar.activation(out=xnTst[:, sl, j, i * 128:(i + 1) * 128],
                                              in_=psT[idx % 3][:], func=AF.Identity).then_inc(a_se, 1)
                    for jq in range(NJ):
                        if jq == 0 and g >= 2:
                            scalar.wait_ge(a_ks[sl], 16 * (g // 2))
                        scalar.wait_ge(a_mk, NJ * g + jq + 1)
                        scalar.activation(out=KTst[:, sl, jq, :], in_=psQK[jq % 2][:],
                                          func=AF.Identity).then_inc(a_ke, 1)
                    for i2 in range(4):
                        for dv in range(NVC):
                            k = i2 * NVC + dv
                            if k == 0 and g >= 2:
                                scalar.wait_ge(a_vs[sl], 16 * (g // 2))
                            scalar.wait_ge(a_mv, 4 * NVC * g + k + 1)
                            scalar.activation(out=Vst[:, sl, i2, dv * 512:dv * 512 + VCW[dv]],
                                              in_=psV[k % 2][:, :VCW[dv]], func=AF.Identity).then_inc(a_vse, 1)
                    if g + 1 < NG:
                        sqln(g + 1)
                gexps(NG - 1)

            @blkA.tensor
            def _(tensor):
                def routing_mms(gg):
                    tensor.wait_ge(b_rv, r1done[gg])
                    if gg >= 2:
                        tensor.wait_ge(b_rv, r2done[gg - 2])
                    tensor.wait_ge(a_ge, gg + 1)
                    tensor.matmul(out=psG[gg % 2][:, 4, :], lhsT=uptri_sb[:], rhs=totb[:, gg % 4],
                                  start=True, stop=True)
                    tensor.matmul(out=psG[gg % 2][:, 5, :], lhsT=binm_sb[:, KSUB - 1, 0:128],
                                  rhs=totb[:, gg % 4],
                                  start=True, stop=True).then_inc(b_rm, 1)

                tensor.wait_ge(a_ld, 16 * 9)
                for g in range(NG):
                    sl = g % 2
                    if g >= 3:
                        routing_mms(g - 3)
                    for i in range(4):
                        t = 4 * g + i
                        tensor.wait_ge(a_vx, t + 1)
                        for j in range(NJ):
                            idx = t * NJ + j
                            if idx >= 3:
                                tensor.wait_ge(a_se, idx - 2)
                            tensor.transpose(out=psT[idx % 3][:], in_=xnt[:, t % 8, j * 128:(j + 1) * 128],
                                             identity=ident_sb[:]).then_inc(a_tp, 1)
                    tensor.wait_ge(a_se, 4 * NJ * (g + 1))
                    if g >= 2:
                        tensor.wait_ge(a_ge, g - 1)
                    for i2 in range(4):
                        for j in range(NJ):
                            mi = tensor.matmul(out=psG[g % 2][:, i2, :],
                                               lhsT=xnTst[:, sl, j, i2 * 128:(i2 + 1) * 128],
                                               rhs=WGsb[:, j, :], start=(j == 0), stop=(j == NJ - 1))
                    mi.then_inc(a_mg, 1)
                    for jq in range(NJ):
                        psqk_prior_wait(tensor, g, "q", jq)
                        for j in range(NJ):
                            mi = tensor.matmul(out=psQK[jq % 2][:], lhsT=WQsb[:, j, jq * 128:(jq + 1) * 128],
                                               rhs=xnTst[:, sl, j, :], start=(j == 0), stop=(j == NJ - 1))
                        mi.then_inc(a_mq, 1)
                    for jq in range(NJ):
                        psqk_prior_wait(tensor, g, "k", jq)
                        for j in range(NJ):
                            mi = tensor.matmul(out=psQK[jq % 2][:], lhsT=WKsb[:, j, jq * 128:(jq + 1) * 128],
                                               rhs=xnTst[:, sl, j, :], start=(j == 0), stop=(j == NJ - 1))
                        mi.then_inc(a_mk, 1)
                    for i2 in range(4):
                        for dv in range(NVC):
                            vix = 4 * NVC * g + i2 * NVC + dv
                            if vix >= 2:
                                tensor.wait_ge(a_vse, vix - 1)
                            for j in range(NJ):
                                mi = tensor.matmul(out=psV[(i2 * NVC + dv) % 2][:, :VCW[dv]],
                                                   lhsT=xnTst[:, sl, j, i2 * 128:(i2 + 1) * 128],
                                                   rhs=WVsb[:, j, dv * 512:dv * 512 + VCW[dv]],
                                                   start=(j == 0), stop=(j == NJ - 1))
                            mi.then_inc(a_mv, 1)
                routing_mms(NG - 3)
                routing_mms(NG - 2)
                routing_mms(NG - 1)

        with ExitStack() as pb:
            PB = pb.enter_context
            NSL = 3
            xgT = PB(nc.sbuf_tensor("xgT", [128, 2, NJ, CH], BF16))
            w1r = PB(nc.sbuf_tensor("w1r", [128, NSL, NJ, 256], BF16))
            w3r = PB(nc.sbuf_tensor("w3r", [128, NSL, NJ, 256], BF16))
            w2sb = PB(nc.sbuf_tensor("w2sb", [128, HC, D], BF16))
            hT = PB(nc.sbuf_tensor("hT", [128, HC, CH], BF16))
            s1b = PB(nc.sbuf_tensor("s1b", [128, 2, 512], BF16))
            ysb = PB(nc.sbuf_tensor("ysb", [128, T128, D], F32))
            pg1 = [PB(nc.psum_tensor(f"pg1_{i}", [128, 512], F32)) for i in range(2)]
            pg3 = [PB(nc.psum_tensor(f"pg3_{i}", [128, 512], F32)) for i in range(2)]
            pw = [PB(nc.psum_tensor(f"pw_{i}", [128, 512], F32)) for i in range(2)]
            psT2 = [PB(nc.psum_tensor(f"psT2_{i}", [128, 128], BF16)) for i in range(2)]
            blkB = PB(nc.Block())

            NBW = min(NDC, 2)
            CHWs = [w for (_, _, w) in cfg.chunks]
            T128c = [w // 128 for w in CHWs]
            NTBc = [(w + 511) // 512 for w in CHWs]
            TBWc = [[min(512, w - 512 * i) for i in range(NTBc[c])]
                    for c, w in enumerate(CHWs)]
            tilebase, gbase, wbase = [0], [0], [0]
            for c in range(NCHS):
                tilebase.append(tilebase[-1] + T128c[c])
                gbase.append(gbase[-1] + HC * NTBc[c])
                wbase.append(wbase[-1] + T128c[c] * NDC)
            w1b_r = [w1b_d[e].rearrange("(j p) h -> p j h", p=128) for e in range(E)]
            w3b_r = [w3b_d[e].rearrange("(j p) h -> p j h", p=128) for e in range(E)]
            w2b_r = [w2b_d[e].rearrange("(hc p) d -> p hc d", p=128) for e in range(E)]

            def gidx(c, hc, tb):
                return gbase[c] + hc * NTBc[c] + tb

            def widx(c, t, dc):
                return wbase[c] + t * NDC + dc

            slab_events = []
            for c in range(NCHS):
                e = cfg.chunks[c][0]
                for s, (lo, hi) in enumerate(cfg.slabs):
                    slab_events.append((c, e, s, lo, hi))

            def slab_done_count(si):
                (c, e, s, lo, hi) = slab_events[si]
                return 2 * (gidx(c, hi - 1, NTBc[c] - 1) + 1)

            tr_items, KPT = {}, {}
            for c in range(NCHS):
                if c + 1 < NCHS:
                    items = [(g8, j) for g8 in range(tilebase[c + 1], tilebase[c + 2])
                             for j in range(NJ)]
                else:
                    items = []
                tr_items[c] = items
                npts = max(1, (NSLAB * 2 * NTBc[c]) * 3 // 4)
                KPT[c] = (len(items) + npts - 1) // npts if items else 0


            @blkB.vector
            def _(vector):
                for c in range(NCHS):
                    if c >= 1:
                        vector.wait_ge(b_mW, wbase[c])
                    for hc in range(HC):
                        for tb in range(NTBc[c]):
                            gx = gidx(c, hc, tb)
                            vector.wait_ge(b_mG, 2 * gx + 2)
                            vector.wait_ge(b_sS, gx + 1)
                            vector.tensor_tensor(out=hT[:, hc, tb * 512:tb * 512 + TBWc[c][tb]],
                                                 in0=s1b[:, gx % 2, :TBWc[c][tb]],
                                                 in1=pg3[gx % 2][:, :TBWc[c][tb]],
                                                 op=ALU.mult).then_inc(b_vH, 1)
                    if c >= 1:
                        vector.wait_ge(b_yst, 16 * c)
                        vector.wait_ge(b_vY, wbase[c])
                    for t in range(T128c[c]):
                        for dc in range(NDC):
                            wx = widx(c, t, dc)
                            vector.wait_ge(b_mW, wx + 1)
                            vector.tensor_copy(out=ysb[:, t, dc * 512:dc * 512 + DCW[dc]],
                                               in_=pw[dc % NBW][:, :DCW[dc]]).then_inc(b_vY, 1)

            @blkB.tensor
            def _(tensor):
                def one_tr(g8, j):
                    tix = g8 * NJ + j
                    if j == 0:
                        tensor.wait_ge(b_xgl[g8 % 2], 16 * (g8 // 2 + 1))
                    if tix >= 2:
                        tensor.wait_ge(b_se2, tix - 1)
                    tensor.transpose(out=psT2[tix % 2][:],
                                     in_=tstage[:, g8 % 4, j * 128:(j + 1) * 128],
                                     identity=ident_sb[:]).then_inc(b_mT2, 1)

                for c in range(NCHS):
                    e = cfg.chunks[c][0]
                    if c == 0:
                        for g8 in range(tilebase[0], tilebase[1]):
                            for j in range(NJ):
                                one_tr(g8, j)
                    tensor.wait_ge(b_se2, NJ * tilebase[c + 1])
                    ti = [0]

                    def drip():
                        items = tr_items[c]
                        for _ in range(KPT[c]):
                            if ti[0] < len(items):
                                g8, j = items[ti[0]]
                                ti[0] += 1
                                one_tr(g8, j)

                    for s, (lo, hi) in enumerate(cfg.slabs):
                        si = c * NSLAB + s
                        slot = si % NSL
                        tensor.wait_ge(b_w1f[slot], 16 * (si // NSL + 1))
                        tensor.wait_ge(b_w3f[slot], 16 * (si // NSL + 1))
                        for hc in range(lo, hi):
                            off = (hc - lo) * 128
                            for tb in range(NTBc[c]):
                                gx = gidx(c, hc, tb)
                                if gx >= 2:
                                    tensor.wait_ge(b_sS, gx - 1)
                                for j in range(NJ):
                                    mi = tensor.matmul(out=pg1[gx % 2][:, :TBWc[c][tb]],
                                                       lhsT=w1r[:, slot, j, off:off + 128],
                                                       rhs=xgT[:, c % 2, j, tb * 512:tb * 512 + TBWc[c][tb]],
                                                       start=(j == 0), stop=(j == NJ - 1))
                                mi.then_inc(b_mG, 1)
                                if gx >= 2:
                                    tensor.wait_ge(b_vH, gx - 1)
                                for j in range(NJ):
                                    mi = tensor.matmul(out=pg3[gx % 2][:, :TBWc[c][tb]],
                                                       lhsT=w3r[:, slot, j, off:off + 128],
                                                       rhs=xgT[:, c % 2, j, tb * 512:tb * 512 + TBWc[c][tb]],
                                                       start=(j == 0), stop=(j == NJ - 1))
                                mi.then_inc(b_mG, 1)
                                drip()
                    while ti[0] < len(tr_items[c]):
                        g8, j = tr_items[c][ti[0]]
                        ti[0] += 1
                        one_tr(g8, j)
                    tensor.wait_ge(b_vH, gbase[c + 1])
                    tensor.wait_ge(b_w2f, 32 * (e + 1))
                    for t in range(T128c[c]):
                        for dc in range(NDC):
                            wx = widx(c, t, dc)
                            if wx >= NBW:
                                tensor.wait_ge(b_vY, wx - NBW + 1)
                            for hc in range(HC):
                                mi = tensor.matmul(out=pw[dc % NBW][:, :DCW[dc]],
                                                   lhsT=hT[:, hc, t * 128:(t + 1) * 128],
                                                   rhs=w2sb[:, hc, dc * 512:dc * 512 + DCW[dc]],
                                                   start=(hc == 0), stop=(hc == HC - 1))
                            mi.then_inc(b_mW, 1)

            @blkB.scalar
            def _(scalar):
                def one_ev(cc, g8, j):
                    i = g8 - tilebase[cc]
                    tix = g8 * NJ + j
                    if i == 0 and j == 0 and cc >= 2:
                        scalar.wait_ge(b_mG, 2 * gbase[cc - 1])
                    scalar.wait_ge(b_mT2, tix + 1)
                    scalar.activation(out=xgT[:, cc % 2, j, i * 128:(i + 1) * 128],
                                      in_=psT2[tix % 2][:], func=AF.Identity).then_inc(b_se2, 1)

                for c in range(NCHS):
                    if c == 0:
                        for g8 in range(tilebase[0], tilebase[1]):
                            for j in range(NJ):
                                one_ev(0, g8, j)
                    si_ = [0]

                    def sdrip():
                        items = tr_items[c]
                        for _ in range(KPT[c]):
                            if si_[0] < len(items):
                                g8, j = items[si_[0]]
                                si_[0] += 1
                                one_ev(c + 1, g8, j)

                    for hc in range(HC):
                        for tb in range(NTBc[c]):
                            gx = gidx(c, hc, tb)
                            scalar.wait_ge(b_mG, 2 * gx + 1)
                            if gx >= 2:
                                scalar.wait_ge(b_vH, gx - 1)
                            scalar.activation(out=s1b[:, gx % 2, :TBWc[c][tb]],
                                              in_=pg1[gx % 2][:, :TBWc[c][tb]],
                                              func=AF.Silu).then_inc(b_sS, 1)
                            sdrip()
                            if (cfg.chunks[c][1] == 0 and tb == 0
                                    and hc == max(0, HC - 8)):
                                e_ = cfg.chunks[c][0]
                                if c >= 1:
                                    scalar.wait_ge(b_mW, wbase[c])
                                half = (HC + 1) // 2
                                scalar.dma_start(out=w2sb[:, :half, :],
                                                 in_=w2b_r[e_][:, :half, :]).then_inc(b_w2f, 16)
                                scalar.dma_start(out=w2sb[:, half:, :],
                                                 in_=w2b_r[e_][:, half:, :]).then_inc(b_w2f, 16)
                    while si_[0] < len(tr_items[c]):
                        g8, j = tr_items[c][si_[0]]
                        si_[0] += 1
                        one_ev(c + 1, g8, j)

            @blkB.gpsimd
            def _(g):
                half = (HC + 1) // 2
                npc = {e_: (len(cast_pieces[e_]) if e_ < 2 else 0) for e_ in range(E)}
                seen = set()

                def drip(e_):
                    if e_ < E and npc[e_] < len(cast_pieces[e_]):
                        o_, i_ = cast_pieces[e_][npc[e_]]
                        g.dma_start(out=o_, in_=i_).then_inc(b_cast[e_ % 4], 16)
                        npc[e_] += 1
                        return True
                    return False

                def tile_load(g8):
                    if g8 >= 4:
                        g.wait_ge(b_mT2, NJ * (g8 - 3))
                    g.dma_start(out=tstage[:, g8 % 4, :],
                                in_=xg_d[g8 * 128:(g8 + 1) * 128, :]
                                ).then_inc(b_xgl[g8 % 2], 16)

                for si, (c, e, s, lo, hi) in enumerate(slab_events):
                    slot = si % NSL
                    wdt = (hi - lo) * 128
                    if e not in seen:
                        seen.add(e)
                        while drip(e):
                            pass
                        nslot = sum(len(cast_pieces[e2]) for e2 in range(e + 1)
                                    if e2 % 4 == e % 4)
                        g.wait_ge(b_cast[e % 4], 16 * nslot)
                    if si >= NSL:
                        g.wait_ge(b_mG, slab_done_count(si - NSL))
                    g.dma_start(out=w1r[:, slot, :, :wdt],
                                in_=w1b_r[e][:, :, lo * 128:hi * 128]).then_inc(b_w1f[slot], 16)
                    g.dma_start(out=w3r[:, slot, :, :wdt],
                                in_=w3b_r[e][:, :, lo * 128:hi * 128]).then_inc(b_w3f[slot], 16)
                    if s % 2 == 0:
                        drip(e + 2)
                    if si == 0:
                        for s4 in range(4):
                            g.wait_ge(b_sc[s4], 32 * ((NT - s4 + 3) // 4))
                        for g8 in range(tilebase[0], tilebase[1]):
                            tile_load(g8)
                    if c + 1 < NCHS and s < T128c[c + 1]:
                        tile_load(tilebase[c + 1] + s)

            @blkB.sync
            def _(sync):
                for c in range(NCHS):
                    sync.wait_ge(b_vY, wbase[c + 1])
                    sync.dma_start(out=ys_r[:, tilebase[c]:tilebase[c + 1], :],
                                   in_=ysb[:, :T128c[c], :]).then_inc(b_yst, 16)
                sync.wait_ge(b_yst, 16 * NCHS)

        pab.close()
        with ExitStack() as pc:
            PC = pc.enter_context
            QB2 = 256
            QI = QB2 // 128
            NQB2 = S // QB2
            VD = 6
            QTsb = PC(nc.sbuf_tensor("QTsb", [128, 2, NJ, QB2], BF16))
            kres = PC(nc.sbuf_tensor("kres", [128, NJ, S], BF16))
            vtr = PC(nc.sbuf_tensor("vtr", [128, VD, D], BF16))
            PTs = PC(nc.sbuf_tensor("PTs", [128, 4, QB2], BF16))
            binmT_sb = PC(nc.sbuf_tensor("binmT_sb", [128, 2, QB2], BF16))
            ones_sb = PC(nc.sbuf_tensor("ones_sb", [128, 1], BF16))
            recL = PC(nc.sbuf_tensor("recL", [128, 2], F32))
            xtl = PC(nc.sbuf_tensor("xtl", [128, 2, D], F32))
            ygA = PC(nc.sbuf_tensor("ygA", [128, 2, D], F32))
            ygB = PC(nc.sbuf_tensor("ygB", [128, 2, D], F32))
            ot1 = PC(nc.sbuf_tensor("ot1", [128, 2, D], F32))
            ot2 = PC(nc.sbuf_tensor("ot2", [128, D], F32))
            otl = PC(nc.sbuf_tensor("otl", [128, 2, D], F32))
            psS = [PC(nc.psum_tensor(f"psS_{i}", [128, QB2], F32)) for i in range(2)]
            psO = [PC(nc.psum_tensor(f"psO_{i}", [128, 512], F32)) for i in range(4)]
            psL = [PC(nc.psum_tensor(f"psL_{i}", [128, 8], F32)) for i in range(2)]
            blkC = PC(nc.Block())

            pairs = []
            for qb in range(NQB2):
                for kt in range(2 * (qb + 1)):
                    pairs.append((qb, kt))
            p_ord = {ev: i for i, ev in enumerate(pairs)}
            pairs_end = {}
            tot = 0
            for qb in range(NQB2):
                tot += 2 * (qb + 1)
                pairs_end[qb] = tot
            kt_slot = {ev: i % VD for i, ev in enumerate(pairs)}
            kt_sem = {ev: i % 4 for i, ev in enumerate(pairs)}
            kt_fill = {ev: 16 * (i // 4 + 1) for i, ev in enumerate(pairs)}
            mask_cum = []
            cnt = 0
            for (qb, kt) in pairs:
                if kt >= 2 * qb:
                    cnt += 1
                mask_cum.append(cnt)

            @blkC.sync
            def _(sync):
                for qb in range(NQB2):
                    if qb >= 2:
                        sync.wait_ge(c_mS2, pairs_end[qb - 2])
                    sync.dma_start(out=QTsb[:, qb % 2, :, :],
                                   in_=qT_r[:, :, qb * QB2:(qb + 1) * QB2]
                                   ).then_inc(c_qtl[qb % 2], 16)
                    for kt in range(2 * (qb + 1)):
                        ev = (qb, kt)
                        i = p_ord[ev]
                        slot = kt_slot[ev]
                        if i >= VD:
                            tq, tk = pairs[i - VD]
                            sync.wait_ge(c_mO, p_ord[(tq, tk)] + 1)
                        sync.dma_start(out=vtr[:, slot, :],
                                       in_=vtk_r[:, kt, :]).then_inc(c_vtf[i % 4], 16)

            @blkC.gpsimd
            def _(g):
                g.dma_start(out=binmT_sb[:], in_=binmT_d.rearrange("m p q -> p m q")
                            ).then_inc(c_vMm, 16)
                g.dma_start(out=ones_sb[:], in_=ones_d).then_inc(c_vMm, 16)
                for qb in range(NQB2):
                    for qi in range(QI):
                        eidx = qb * QI + qi
                        slot = eidx % 2
                        if eidx >= 2:
                            g.wait_ge(c_vEp, eidx - 1)
                        g.dma_start(out=xtl[:, slot, :],
                                    in_=x_d[eidx * 128:(eidx + 1) * 128, :]).then_inc(c_xlf[slot], 16)
                        g.indirect_dma_start(out=ygA[:, slot, :], out_offset=None,
                                             in_=ys_d,
                                             in_offset=IndirectOffsetOnAxis(s1i[:, eidx:eidx + 1], 0)
                                             ).then_inc(c_yga[slot], 16)
                        g.indirect_dma_start(out=ygB[:, slot, :], out_offset=None,
                                             in_=ys_d,
                                             in_offset=IndirectOffsetOnAxis(s2i[:, eidx:eidx + 1], 0)
                                             ).then_inc(c_ygb[slot], 16)
                g.wait_ge(c_ost[0], 16 * ((NT + 1) // 2))
                g.wait_ge(c_ost[1], 16 * (NT // 2))

            @blkC.tensor
            def _(tensor):
                PD = 3

                def emit_S(ev):
                    (qb, kt) = ev
                    p = p_ord[ev]
                    li = kt // 4 if kt < 8 else 2 + (kt - 8) // 8
                    tensor.wait_ge(c_kr[li % 4], 16 * (li // 4 + 1))
                    if p >= 2:
                        tensor.wait_ge(c_sE, p - 1)
                    if kt == 0:
                        tensor.wait_ge(c_qtl[qb % 2], 16 * (qb // 2 + 1))
                    isdiag = (kt >= 2 * qb)
                    for j in range(NJ):
                        mi = tensor.matmul(out=psS[p % 2][:],
                                           lhsT=kres[:, j, kt * 128:(kt + 1) * 128],
                                           rhs=QTsb[:, qb % 2, j, :],
                                           start=(j == 0),
                                           stop=(j == NJ - 1 and not isdiag))
                    if isdiag:
                        mi = tensor.matmul(out=psS[p % 2][:], lhsT=ident_sb[:],
                                           rhs=binmT_sb[:, kt - 2 * qb, :],
                                           start=False, stop=True)
                    mi.then_inc(c_mS2, 1)

                def emit_PV(ev):
                    (qb, kt) = ev
                    p = p_ord[ev]
                    nk = 2 * (qb + 1)
                    tensor.wait_ge(c_sE, p + 1)
                    tensor.wait_ge(c_vtf[kt_sem[ev]], kt_fill[ev])
                    qis = [0, 1] if kt < 2 * qb + 1 else [1]
                    for qi in qis:
                        if kt == 0 and qb >= 1:
                            tensor.wait_ge(c_vEr, 6 * (qb - 1) + (2 if qi == 0 else 4))
                        first = (kt == 0)
                        last = (kt == nk - 1) if qi == 1 else (kt == nk - 2)
                        for dv in range(NDC):
                            tensor.matmul(out=psO[2 * qi + dv][:, :DCW[dv]],
                                          lhsT=PTs[:, p % 4, qi * 128:(qi + 1) * 128],
                                          rhs=vtr[:, kt_slot[ev], dv * 512:dv * 512 + DCW[dv]],
                                          start=first, stop=last)
                        mi = tensor.matmul(out=psL[qi][:, 0:1],
                                           lhsT=PTs[:, p % 4, qi * 128:(qi + 1) * 128],
                                           rhs=ones_sb[:],
                                           start=first, stop=last)
                    mi.then_inc(c_mO, 1)

                tensor.wait_ge(c_vMm, 32)
                for p, ev in enumerate(pairs):
                    emit_S(ev)
                    if p >= PD:
                        emit_PV(pairs[p - PD])
                for p in range(len(pairs) - PD, len(pairs)):
                    emit_PV(pairs[p])

            @blkC.scalar
            def _(scalar):
                scalar.wait_ge(a_ks[0], 16 * (NG // 2))
                scalar.wait_ge(a_ks[1], 16 * (NG // 2))
                kspans = [(0, 512), (512, 512)] + [(1024 * i, 1024)
                                                                   for i in range(1, S // 1024)]
                for li, (off, w) in enumerate(kspans):
                    scalar.dma_start(out=kres[:, :, off:off + w],
                                     in_=kT_r[:, :, off:off + w]
                                     ).then_inc(c_kr[li % 4], 16)
                store_after = {}
                for qb in range(NQB2):
                    for qi in range(QI):
                        eidx = qb * QI + qi
                        trig = (p_ord[(qb + 1, min(3 + qi, 2 * (qb + 1) + 1))]
                                if qb + 1 < NQB2 else None)
                        store_after.setdefault(trig, []).append(eidx)

                def emit_store(eidx):
                    scalar.wait_ge(c_vEp, eidx + 1)
                    scalar.dma_start(out=out_d[eidx * 128:(eidx + 1) * 128, :],
                                     in_=otl[:, eidx % 2, :]).then_inc(c_ost[eidx % 2], 16)

                for (qb, kt) in pairs:
                    p = p_ord[(qb, kt)]
                    scalar.wait_ge(c_mS2, p + 1)
                    if p >= 4:
                        scalar.wait_ge(c_mO, p - 3)
                    scalar.activation(out=PTs[:, p % 4, :], in_=psS[p % 2][:],
                                      func=AF.Exp, bias=fc_sb[:, 1:2]).then_inc(c_sE, 1)
                    for eidx in store_after.get(p, []):
                        emit_store(eidx)
                for eidx in store_after.get(None, []):
                    emit_store(eidx)

            @blkC.vector
            def _(vector):
                for qb in range(NQB2):
                    vector.wait_ge(c_mO, pairs_end[qb])
                    for qi in range(QI):
                        eidx = qb * QI + qi
                        if qb >= 1:
                            vector.wait_ge(c_vEr, 6 * (qb - 1) + (2 if qi == 0 else 4))
                        vector.reciprocal(out=recL[:, qi:qi + 1],
                                          in_=psL[qi][:, 0:1]).then_inc(c_vEr, 1)
                        vector.wait_ge(c_vEr, 6 * qb + (1 if qi == 0 else 3))
                        vector.wait_ge(c_xlf[eidx % 2], 16 * (eidx // 2 + 1))
                        for dv in range(NDC):
                            ins = vector.scalar_tensor_tensor(
                                out=ot1[:, qi, dv * 512:dv * 512 + DCW[dv]],
                                in0=psO[2 * qi + dv][:, :DCW[dv]], scalar=recL[:, qi:qi + 1],
                                in1=xtl[:, eidx % 2, dv * 512:dv * 512 + DCW[dv]],
                                op0=ALU.mult, op1=ALU.add)
                        ins.then_inc(c_vEr, 1)
                    for qi in range(QI):
                        eidx = qb * QI + qi
                        vector.wait_ge(c_vEr, 6 * qb + (2 if qi == 0 else 4))
                        if qi == 0 and qb >= 1:
                            vector.wait_ge(c_vEp, 2 * (qb - 1) + 2)
                        if qi == 1:
                            vector.wait_ge(c_vEp, 2 * qb + 1)
                        vector.wait_ge(c_yga[eidx % 2], 16 * (eidx // 2 + 1))
                        vector.scalar_tensor_tensor(out=ot2[:], in0=ygA[:, eidx % 2, :],
                                                    scalar=cwA[:, eidx:eidx + 1], in1=ot1[:, qi, :],
                                                    op0=ALU.mult, op1=ALU.add).then_inc(c_vEr, 1)
                        vector.wait_ge(c_vEr, 6 * qb + (5 if qi == 0 else 6))
                        vector.wait_ge(c_ygb[eidx % 2], 16 * (eidx // 2 + 1))
                        if eidx >= 2:
                            vector.wait_ge(c_ost[eidx % 2], 16 * (eidx // 2))
                        vector.scalar_tensor_tensor(out=otl[:, eidx % 2, :], in0=ygB[:, eidx % 2, :],
                                                    scalar=cwB[:, eidx:eidx + 1], in1=ot2[:],
                                                    op0=ALU.mult, op1=ALU.add).then_inc(c_vEp, 1)

    return nc




CFG = Cfg()


def _prep_inputs(L0_attn_embeddings, L0_attn_W_Q, L0_attn_W_K, L0_attn_W_V,
                 L0_ffn_Wg, L0_ffn_W1, L0_ffn_W2, L0_ffn_W3, cfg):
    KSUB = cfg.KSUB
    E, C = cfg.E, cfg.CCAP
    w1t = np.ascontiguousarray(np.swapaxes(np.asarray(L0_ffn_W1), 1, 2))
    w3t = np.ascontiguousarray(np.swapaxes(np.asarray(L0_ffn_W3), 1, 2))
    w2t = np.ascontiguousarray(np.swapaxes(np.asarray(L0_ffn_W2), 1, 2))
    wgt = np.ascontiguousarray(np.asarray(L0_ffn_Wg).T)
    ident = np.eye(128, dtype=np.float32).astype(ml_dtypes.bfloat16)
    kk = np.arange(128)[:, None]
    mm = np.arange(128)[None, :]
    uptri = (kk < mm).astype(np.float32).astype(ml_dtypes.bfloat16)
    binm = np.zeros((KSUB, 128, cfg.KW), dtype=np.float32)
    ii = np.arange(128)[:, None]
    jj = np.arange(cfg.KW)[None, :]
    for m in range(KSUB):
        cc = jj // 128
        off = jj % 128
        binm[m] = ((cc < m) | ((cc == m) & (off <= ii))).astype(np.float32)
    binm = binm.astype(ml_dtypes.bfloat16)
    kk2 = np.arange(128)[:, None]
    qq2 = np.arange(256)[None, :]
    binmT = np.stack([-50.0 * (qq2 < kk2 + 128 * m) for m in range(2)]).astype(np.float32)
    binmT = binmT.astype(ml_dtypes.bfloat16)
    ee = np.arange(E, dtype=np.float32)
    econst = np.zeros((128, 4, E), dtype=np.float32)
    econst[:, 0, :] = ee * C
    econst[:, 1, :] = (ee + 1) * C
    econst[:, 2, :] = E * C
    econst[:, 3, :] = 1e9
    common = {
        "wq": np.ascontiguousarray(np.asarray(L0_attn_W_Q)),
        "wk": np.ascontiguousarray(np.asarray(L0_attn_W_K)),
        "wv": np.ascontiguousarray(np.asarray(L0_attn_W_V)),
        "wgt": wgt, "w1t": w1t, "w3t": w3t, "w2t": w2t,
        "identb": ident, "uptri": uptri, "binm": binm, "econst": econst,
        "binmT": binmT, "ones1": np.ones((128, 1), np.float32).astype(ml_dtypes.bfloat16),
        "fconst": np.tile(np.array([[-0.5, -16.0]], dtype=np.float32), (128, 1)),
    }
    emb = np.asarray(L0_attn_embeddings)
    in_maps = []
    for b in range(emb.shape[0]):
        m = dict(common)
        m["x"] = np.ascontiguousarray(emb[b])
        in_maps.append(m)
    return in_maps


def kernel(L0_attn_embeddings, L0_attn_W_Q, L0_attn_W_K, L0_attn_W_V,
           L0_ffn_Wg, L0_ffn_W1, L0_ffn_W2, L0_ffn_W3):
    cfg = CFG
    in_maps = _prep_inputs(L0_attn_embeddings, L0_attn_W_Q, L0_attn_W_K,
                           L0_attn_W_V, L0_ffn_Wg, L0_ffn_W1, L0_ffn_W2,
                           L0_ffn_W3, cfg)
    nc = build_program(cfg)
    res = run_bass_kernel_spmd(nc, in_maps, core_ids=list(range(len(in_maps))))
    out = np.stack([np.asarray(r["out"]) for r in res.results], axis=0)
    return out.astype(np.float32)
